# version 11
# baseline (speedup 1.0000x reference)
import numpy as np
import concourse.bass as bass
import concourse.mybir as mybir
from concourse.bass_utils import run_bass_kernel_spmd
from concourse.libnrt import node_relative_to_routing_address

F32 = mybir.dt.float32
BF16 = mybir.dt.bfloat16
OP = mybir.AluOpType
AF = mybir.ActivationFunctionType
AX = mybir.AxisListType.X

NCORE = 8
TOK = 2048
NCH = 16
D = 1024
FF = 4096
HD = 64
NPAIR = 4
KCH = 64
NKCH = 64
EPS_LN = 1e-5
EPS_TTT = 1e-6
EPS_ATT = 1e-6

_UPTO = 6
_TSTAGE = 5
_TITER = None
_TPHASE = 9
_TSEL = "all"

_ENGINES = ("sync", "tensor", "vector", "scalar", "gpsimd")


class Plan:
    def __init__(self):
        self.ops = {e: [] for e in _ENGINES}
        self.cnt = {}

    def raw(self, eng, fname, *args, inc=(), **kw):
        for s, a in inc:
            self.cnt[s] = self.cnt.get(s, 0) + a
        self.ops[eng].append((fname, args, kw, tuple(inc)))

    def wait(self, eng, sem, val):
        self.ops[eng].append(("__wait__", (sem, val), {}, ()))

    def drain(self, eng, inc=()):
        self.raw(eng, "__drain__", inc=inc)

    def bar(self, eng, sem):
        self.drain(eng, inc=((sem, 1),))
        return self.cnt[sem]

    def dma(self, eng, out, in_, sem):
        self.raw(eng, "dma_start", inc=((sem, 16),), out=out, in_=in_)
        return self.cnt[sem]

    def custom(self, eng, fn, inc=()):
        for s, a in inc:
            self.cnt[s] = self.cnt.get(s, 0) + a
        self.ops[eng].append(("__custom__", (fn,), {}, tuple(inc)))

    def val(self, sem):
        return self.cnt.get(sem, 0)


def _exec_ops(eng_obj, oplist, sems):
    for fn, args, kw, inc in oplist:
        if fn == "__wait__":
            eng_obj.wait_ge(sems[args[0]], args[1])
            continue
        if fn == "__drain__":
            inst = eng_obj.drain()
        elif fn == "__custom__":
            args[0](eng_obj, sems)
            inst = None
        else:
            inst = getattr(eng_obj, fn)(*args, **kw)
        if inst is not None:
            for s, a in inc:
                inst.then_inc(sems[s], a)


def build_program(upto, tstage=5, titer=None, tphase=9, tsel='all'):
    nc = bass.Bass("TRN2", target_bir_lowering=False)

    def In(name, shape, dtype=F32):
        return nc.dram_tensor(name, list(shape), dtype, kind="ExternalInput").ap()

    def Internal(name, shape, dtype=F32):
        return nc.dram_tensor(name, list(shape), dtype, kind="Internal").ap()

    x_in = In("x_in", [NCH, 128, D])
    ident_h = In("ident", [128, 128])
    triu_h = In("triu", [128, 128])
    wfcT = In("wfcT", [32, 8, 128, 128])
    wprojT = In("wprojT", [32, 128, D])
    w1T = In("w1T", [32, 8, 128, 128])
    w2T = In("w2T", [32, 128, D])
    routerT = In("routerT", [8, 128, 16])
    rb_rep = In("rb_rep", [128, 16])
    b1cols = In("b1cols", [128, 32])
    b2rep = In("b2rep", [128, D])
    wqkvT = In("wqkvT", [3, 8, 8, 128, 128])
    wo_r = In("wo_r", [8, 128, D])
    awqkvT = In("awqkvT", [3, 8, 8, 128, 128])
    awoT = In("awoT", [8, 128, D])
    ttt_W0 = In("ttt_W0", [NPAIR, 128, 64])
    ttt_b0f = In("ttt_b0f", [64, NPAIR, 128])
    ttt_th = In("ttt_th", [128, NPAIR])
    ttt_tbf = In("ttt_tbf", [64, 2 * NPAIR])
    ttt_gf = In("ttt_gf", [64, NPAIR, 128])
    ttt_bef = In("ttt_bef", [64, NPAIR, 128])
    ttt_al = In("ttt_al", [64, 1])
    ttt_ik = In("ttt_ik", [64, 1])
    cvals_h = In("cvals", [128, 12])

    y_out = nc.dram_tensor("y", [NCH, 128, D], F32, kind="ExternalOutput").ap()

    x_spill = Internal("x_spill", [NCH, 128, D])
    wfcT_h = Internal("wfcT_h", [32, 8, 128, 128], BF16)
    wfcT_l = Internal("wfcT_l", [32, 8, 128, 128], BF16)
    wprojT_h = Internal("wprojT_h", [32, 128, D], BF16)
    wprojT_l = Internal("wprojT_l", [32, 128, D], BF16)
    w1T_h = Internal("w1T_h", [32, 8, 128, 128], BF16)
    w1T_l = Internal("w1T_l", [32, 8, 128, 128], BF16)
    w2T_h = Internal("w2T_h", [32, 128, D], BF16)
    w2T_l = Internal("w2T_l", [32, 128, D], BF16)
    routerT_h = Internal("routerT_h", [8, 128, 16], BF16)
    routerT_l = Internal("routerT_l", [8, 128, 16], BF16)
    wqkvT_h = Internal("wqkvT_h", [3, 8, 8, 128, 128], BF16)
    wqkvT_l = Internal("wqkvT_l", [3, 8, 8, 128, 128], BF16)
    awqkvT_h = Internal("awqkvT_h", [3, 8, 8, 128, 128], BF16)
    awqkvT_l = Internal("awqkvT_l", [3, 8, 8, 128, 128], BF16)
    wo_r_h = Internal("wo_r_h", [8, 128, D], BF16)
    wo_r_l = Internal("wo_r_l", [8, 128, D], BF16)
    awoT_h = Internal("awoT_h", [8, 128, D], BF16)
    awoT_l = Internal("awoT_l", [8, 128, D], BF16)
    cc_a_in = Internal("cc_a_in", [3, NPAIR, 128, TOK])
    cc_a_out = Internal("cc_a_out", [2, 3, NPAIR, 128, TOK])
    cc_b_in = Internal("cc_b_in", [NPAIR, 128, TOK])
    cc_b_out = Internal("cc_b_out", [2, NPAIR, 128, TOK])
    own_a = Internal("own_a", [3, NPAIR, 128, TOK])
    kv1 = Internal("kv1", [3, NPAIR, 128, 2 * TOK])
    own_b = Internal("own_b", [NPAIR, 128, 2 * TOK])
    ocanon = Internal("ocanon", [8, 128, TOK])

    sbuf = lambda name, shape, dtype=F32: nc.alloc_sbuf_tensor(name, list(shape), dtype).ap()
    X = sbuf("X", [128, NCH, D])
    BIG = sbuf("BIG", [128, 8, TOK])
    WBUF = sbuf("WBUF", [128, 8192])
    ident = sbuf("ident_s", [128, 128])
    triu = sbuf("triu_s", [128, 128])
    xn = sbuf("xn", [128, 1, D])
    stats = sbuf("stats", [128, 128])
    hq = sbuf("hq", [128, 1024])
    hr = sbuf("hr", [128, 512])
    stage = sbuf("stage", [128, 1024])
    maskb = sbuf("maskb", [128, NCH, 16])
    tcol = sbuf("tcol", [128, 32])
    b1s = sbuf("b1s", [128, 32])
    attnb = sbuf("attnb", [128, 512])
    b2s = sbuf("b2s", [128, D])
    sA = BIG[:, 0, 0:1024]
    sB = BIG[:, 1, 0:1024]
    sC = BIG[:, 2, 0:1024]
    sD = stage[:, 0:1024]
    tsc = BIG[:, 3, 0:1024].rearrange("p (c d) -> p c d", d=64)
    Wst = sbuf("Wst", [128, NPAIR, 64])
    bstf = sbuf("bstf", [64, NPAIR, 128])
    tgf = sbuf("tgf", [64, NPAIR, 128])
    tbef = sbuf("tbef", [64, NPAIR, 128])
    tth = sbuf("tth", [128, NPAIR])
    ttbf = sbuf("ttbf", [64, 2 * NPAIR])
    eta0 = sbuf("eta0", [64, 1])
    tmw = BIG[:, 3, :]
    odsb = BIG[:, 4, 0:64]
    WB16 = WBUF.bitcast(BF16)
    BIG16 = BIG.bitcast(BF16)
    XTh = BIG16[:, :, 0:2048]
    XTl = BIG16[:, :, 2048:4096]
    qkvb = WBUF[:, 0:6144].rearrange("p (q r t) -> p q r t", q=3, r=NPAIR)
    kpb = WBUF[:, 0:4096]
    vpb = WBUF[:, 4096:8192]
    qpb = BIG[:, 4:6, :].rearrange("p a t -> p (a t)")
    vtk = BIG[:, 0:3, :].rearrange("p a t -> p (a t)")[:, 0:4160] \
        .rearrange("p (c t) -> p c t", t=130)

    PS = [nc.alloc_psum_tensor(f"ps{i}", [128, 512], F32).ap() for i in range(8)]

    cvals = sbuf("cvals_s", [128, 12])
    _CONSTS = [EPS_LN, EPS_TTT, -1.0, 1.0, 0.0, 0.5, 0.1, 0.9, 0.7071067811865476]
    for _i, _v in enumerate(_CONSTS):
        nc.const_aps.register(F32, _v, cvals[:, _i:_i + 1])

    p = Plan()
    semnames = ["sIN", "sWT", "sT", "sV", "sS", "sG", "sSY", "sCC", "sOUT"]

    p.dma("sync", cvals[:], cvals_h, "sIN")
    p.dma("sync", b1s[:], b1cols, "sIN")
    p.dma("sync", b2s[:], b2rep, "sIN")
    p.dma("sync", X[:], x_in.rearrange("c p d -> p c d"), "sIN")
    p.dma("sync", ident[:], ident_h, "sIN")
    p.dma("sync", triu[:], triu_h, "sIN")
    p.dma("sync", Wst[:], ttt_W0.rearrange("q p d -> p q d"), "sIN")
    p.dma("sync", bstf[:], ttt_b0f.rearrange("t q d -> t q d"), "sIN")
    p.dma("sync", tgf[:], ttt_gf, "sIN")
    p.dma("sync", tbef[:], ttt_bef, "sIN")
    p.dma("sync", tth[:], ttt_th, "sIN")
    p.dma("sync", ttbf[:], ttt_tbf, "sIN")
    p.dma("sync", tcol[0:64, 0:1], ttt_al, "sIN")
    p.dma("sync", tcol[0:64, 1:2], ttt_ik, "sIN")
    p.wait("vector", "sIN", p.val("sIN"))
    p.raw("vector", "tensor_tensor", eta0[:], tcol[0:64, 0:1], tcol[0:64, 1:2], OP.add)
    p.drain("vector")
    p.raw("vector", "tensor_scalar_max", eta0[:], eta0[:], 0.0)
    p.bar("vector", "sV")

    def pregen():
        stg = WBUF[:, 0:4096]
        hi_t = WB16[:, 8192:12288]
        lo_t = WB16[:, 12288:16384]
        jobs = [(wfcT, wfcT_h, wfcT_l), (wprojT, wprojT_h, wprojT_l),
                (w1T, w1T_h, w1T_l), (w2T, w2T_h, w2T_l),
                (routerT, routerT_h, routerT_l),
                (wqkvT, wqkvT_h, wqkvT_l), (awqkvT, awqkvT_h, awqkvT_l),
                (wo_r, wo_r_h, wo_r_l), (awoT, awoT_h, awoT_l)]
        for srcw, dsth, dstl in jobs:
            n = 1
            for s in srcw.shape:
                n *= s
            cols = min(4096, n // 128)
            nt = n // (128 * cols)
            sf = srcw.flatten().rearrange("(t p c) -> t p c", p=128, c=cols)
            hf = dsth.flatten().rearrange("(t p c) -> t p c", p=128, c=cols)
            lf = dstl.flatten().rearrange("(t p c) -> t p c", p=128, c=cols)
            for t in range(nt):
                p.wait("sync", "sSY", p.val("sSY"))
                ld = p.dma("sync", stg[:, 0:cols], sf[t], "sSY")
                p.wait("vector", "sSY", ld)
                p.raw("vector", "tensor_copy", hi_t[:, 0:cols], stg[:, 0:cols])
                p.drain("vector")
                p.raw("vector", "scalar_tensor_tensor", lo_t[:, 0:cols],
                      stg[:, 0:cols], 1.0, hi_t[:, 0:cols], OP.mult, OP.subtract)
                vd = p.bar("vector", "sV")
                p.wait("sync", "sV", vd)
                p.dma("sync", hf[t], hi_t[:, 0:cols], "sSY")
                p.dma("sync", lf[t], lo_t[:, 0:cols], "sSY")

    pregen()

    def ln_to_xt():
        p.raw("vector", "reduce_sum", stats[:, 16:32], X[:], axis=AX)
        vsum = p.bar("vector", "sV")
        p.wait("scalar", "sV", vsum)
        for c in range(NCH):
            p.raw("scalar", "activation", sD, X[:, c, :], AF.Square,
                  bias=0.0, scale=1.0, accum_out=stats[:, 32 + c:33 + c])
            p.drain("scalar")
        ssq = p.bar("scalar", "sS")
        p.wait("vector", "sS", ssq)
        p.raw("vector", "tensor_scalar_mul", stats[:, 48:64], stats[:, 16:32], 1.0 / D)
        p.raw("vector", "tensor_scalar_mul", stats[:, 64:80], stats[:, 32:48], 1.0 / D)
        p.drain("vector")
        p.raw("vector", "tensor_tensor", stats[:, 80:96], stats[:, 48:64],
              stats[:, 48:64], OP.mult)
        p.drain("vector")
        p.raw("vector", "tensor_tensor", stats[:, 96:112], stats[:, 80:96],
              stats[:, 64:80], OP.subtract)
        vvar = p.bar("vector", "sV")
        p.wait("scalar", "sV", vvar)
        p.raw("scalar", "activation", stats[:, 112:128], stats[:, 96:112], AF.Sqrt,
              bias=EPS_LN, scale=-1.0)
        sstd = p.bar("scalar", "sS")
        p.wait("vector", "sS", sstd)
        p.raw("vector", "reciprocal", stats[:, 16:32], stats[:, 112:128])
        p.raw("vector", "tensor_scalar_mul", stats[:, 48:64], stats[:, 48:64], -1.0)
        p.bar("vector", "sV")
        tr_done = 0
        for c in range(NCH):
            if c >= 1:
                p.wait("vector", "sT", tr_done)
            p.raw("vector", "tensor_scalar", xn[:, 0, :], X[:, c, :],
                  stats[:, 48 + c:49 + c], stats[:, 16 + c:17 + c], OP.add, OP.mult)
            vapp = p.bar("vector", "sV")
            p.wait("tensor", "sV", vapp)
            for w in range(2):
                for d4 in range(4):
                    dk = w * 4 + d4
                    p.raw("tensor", "transpose", PS[d4][:, 0:128],
                          xn[:, 0, dk * 128:(dk + 1) * 128], ident[:])
                td = p.bar("tensor", "sT")
                tr_done = td
                p.wait("vector", "sT", td)
                for d4 in range(4):
                    dk = w * 4 + d4
                    p.raw("vector", "tensor_copy", XTh[:, dk, c * 128:(c + 1) * 128],
                          PS[d4][:, 0:128])
                p.drain("vector")
                for d4 in range(4):
                    dk = w * 4 + d4
                    p.raw("vector", "scalar_tensor_tensor",
                          XTl[:, dk, c * 128:(c + 1) * 128], PS[d4][:, 0:128], 1.0,
                          XTh[:, dk, c * 128:(c + 1) * 128], OP.mult, OP.subtract)
                ev = p.bar("vector", "sV")
                p.wait("tensor", "sV", ev)
        return p.bar("vector", "sV")

    def ffn(layer):
        is_l1 = layer == "L1"
        wfch, wfcl = (w1T_h, w1T_l) if is_l1 else (wfcT_h, wfcT_l)
        wpjh, wpjl = (w2T_h, w2T_l) if is_l1 else (wprojT_h, wprojT_l)
        hq16 = hq.bitcast(BF16)
        if is_l1:
            p.wait("sync", "sT", p.val("sT"))
            p.wait("sync", "sV", p.val("sV"))
            p.wait("sync", "sSY", p.val("sSY"))
            p.dma("sync", WB16[:, 0:128].rearrange("p (k n) -> p k n", n=16),
                  routerT_h.rearrange("k p n -> p k n"), "sWT")
            p.dma("sync", WB16[:, 128:256].rearrange("p (k n) -> p k n", n=16),
                  routerT_l.rearrange("k p n -> p k n"), "sWT")
            p.dma("sync", WBUF[:, 128:144], rb_rep, "sWT")
            wtv = p.val("sWT")
            p.wait("tensor", "sWT", wtv)
            rth = WB16[:, 0:128].rearrange("p (k n) -> p k n", n=16)
            rtl = WB16[:, 128:256].rearrange("p (k n) -> p k n", n=16)
            for c in range(NCH):
                psl = PS[4][:, 0:16]
                p.wait("tensor", "sV", p.val("sV"))
                nmm = 0
                for k in range(8):
                    for xa, wa in ((XTh, rth), (XTh, rtl), (XTl, rth)):
                        nmm += 1
                        p.raw("tensor", "matmul", psl,
                              xa[:, k, c * 128:(c + 1) * 128], wa[:, k, :],
                              start=(nmm == 1), stop=(nmm == 24))
                tl = p.bar("tensor", "sT")
                p.wait("vector", "sT", tl)
                p.raw("vector", "tensor_tensor", maskb[:, c, :], psl,
                      WBUF[:, 128:144], OP.add)
                p.drain("vector")
                l2 = stats[:, 32:48]
                m1 = stats[:, 48:49]
                p.raw("vector", "tensor_copy", l2, maskb[:, c, :])
                p.drain("vector")
                for it in range(3):
                    p.raw("vector", "reduce_max", m1, l2, axis=AX)
                    p.drain("vector")
                    p.raw("vector", "tensor_scalar", stats[:, 49:65], l2,
                          m1[:, 0:1], None, OP.is_ge)
                    p.drain("vector")
                    p.raw("vector", "scalar_tensor_tensor", l2, stats[:, 49:65],
                          -1e30, l2, OP.mult, OP.add)
                    p.drain("vector")
                p.raw("vector", "reduce_max", m1, l2, axis=AX)
                p.drain("vector")
                p.raw("vector", "tensor_scalar", maskb[:, c, :], maskb[:, c, :],
                      m1[:, 0:1], None, OP.is_ge)
                p.bar("vector", "sV")

        p.wait("sync", "sV", p.val("sV"))
        p.wait("sync", "sSY", p.val("sSY"))
        sp = p.dma("sync", x_spill.rearrange("c p d -> p c d"), X[:], "sSY")
        p.wait("vector", "sSY", sp)
        p.wait("tensor", "sSY", sp)

        fcslab = lambda lo: WB16[:, (4096 if lo else 0):(8192 if lo else 4096)]
        pjslab = lambda lo: WB16[:, (12288 if lo else 8192):(16384 if lo else 12288)]
        for fg in range(8):
            p.wait("sync", "sT", p.val("sT"))
            p.wait("sync", "sV", p.val("sV"))
            f0 = fg * 4
            p.dma("sync", fcslab(0).rearrange("p (j k n) -> p j k n", j=4, n=128),
                  wfch[f0:f0 + 4].rearrange("j k p n -> p j k n"), "sWT")
            p.dma("sync", fcslab(1).rearrange("p (j k n) -> p j k n", j=4, n=128),
                  wfcl[f0:f0 + 4].rearrange("j k p n -> p j k n"), "sWT")
            p.dma("sync", pjslab(0).rearrange("p (j n) -> p j n", j=4),
                  wpjh[f0:f0 + 4].rearrange("j p n -> p j n"), "sWT")
            p.dma("sync", pjslab(1).rearrange("p (j n) -> p j n", j=4),
                  wpjl[f0:f0 + 4].rearrange("j p n -> p j n"), "sWT")
            wdone = p.val("sWT")
            for tt in range(8):
                act_done = {}
                nblk = 2 if is_l1 else 1
                for blk in range(nblk):
                    jlist = list(range(blk * (4 // nblk), (blk + 1) * (4 // nblk)))

                    def emit_fc(j):
                        f = f0 + j
                        psh = PS[4 + (j % 2)][:, 0:256]
                        if j == 0:
                            p.wait("tensor", "sWT", wdone)
                        if j == jlist[0]:
                            p.wait("tensor", "sV", p.val("sV"))
                        elif j - 2 in act_done:
                            p.wait("tensor", "sV", act_done[j - 2])
                        nmm = 0
                        for k in range(8):
                            for xa, wsl in ((XTh, 0), (XTh, 1), (XTl, 0)):
                                nmm += 1
                                wt = fcslab(wsl)[:, j * 1024 + k * 128:
                                                 j * 1024 + (k + 1) * 128]
                                p.raw("tensor", "matmul", psh, wt,
                                      xa[:, k, tt * 256:(tt + 1) * 256],
                                      start=(nmm == 1), stop=(nmm == 24))
                        tfc = p.bar("tensor", "sT")
                        p.wait("scalar", "sT", tfc)
                        if is_l1:
                            p.raw("scalar", "activation",
                                  hr[:, (j % 2) * 256:(j % 2) * 256 + 256], psh,
                                  AF.Relu, bias=b1s[:, f:f + 1], scale=1.0)
                        else:
                            p.raw("scalar", "activation",
                                  hr[:, (j % 2) * 256:(j % 2) * 256 + 256], psh,
                                  AF.Relu)
                        sd = p.bar("scalar", "sS")
                        p.wait("vector", "sS", sd)
                        rsl = hr[:, (j % 2) * 256:(j % 2) * 256 + 256]
                        asl = attnb[:, (j % 2) * 256:(j % 2) * 256 + 256]
                        hhi = hq16[:, (j % 2) * 256:(j % 2) * 256 + 256]
                        hlo = hq16[:, 512 + (j % 2) * 256:512 + (j % 2) * 256 + 256]
                        if is_l1:
                            p.raw("vector", "tensor_copy", hhi, rsl)
                            p.drain("vector")
                            p.raw("vector", "scalar_tensor_tensor", hlo, rsl, 1.0,
                                  hhi, OP.mult, OP.subtract)
                        else:
                            p.raw("vector", "tensor_tensor", asl, rsl, rsl, OP.mult)
                            p.drain("vector")
                            p.raw("vector", "tensor_copy", hhi, asl)
                            p.drain("vector")
                            p.raw("vector", "scalar_tensor_tensor", hlo, asl, 1.0,
                                  hhi, OP.mult, OP.subtract)
                        act_done[j] = p.bar("vector", "sV")

                    def emit_proj(ji, j):
                        p.wait("tensor", "sV", act_done[j])
                        hhi = hq16[:, (j % 2) * 256:(j % 2) * 256 + 256]
                        hlo = hq16[:, 512 + (j % 2) * 256:512 + (j % 2) * 256 + 256]
                        first = ji == 0
                        last = ji == len(jlist) - 1
                        nps = 0
                        for ha, wsl in ((hhi, 0), (hhi, 1), (hlo, 0)):
                            nps += 1
                            for sub in range(2):
                                lhs = ha[:, sub * 128:(sub + 1) * 128]
                                for half in range(2):
                                    p.raw("tensor", "matmul", PS[sub * 2 + half],
                                          lhs,
                                          pjslab(wsl)[:, j * 1024 + half * 512:
                                                      j * 1024 + (half + 1) * 512],
                                          start=(first and nps == 1),
                                          stop=(last and nps == 3))

                    for ji, j in enumerate(jlist):
                        emit_fc(j)
                        if ji >= 1:
                            emit_proj(ji - 1, jlist[ji - 1])
                    emit_proj(len(jlist) - 1, jlist[-1])
                    tp = p.bar("tensor", "sT")
                    p.wait("vector", "sT", tp)
                    for sub in range(2):
                        c = tt * 2 + sub
                        for half in range(2):
                            dst = X[:, c, half * 512:(half + 1) * 512]
                            srcp = PS[sub * 2 + half]
                            if is_l1:
                                B = fg * 2 + blk
                                mcol = maskb[:, c, B:B + 1]
                                if fg == 0 and blk == 0:
                                    p.raw("vector", "tensor_scalar", dst, srcp,
                                          mcol, None, OP.mult)
                                else:
                                    p.raw("vector", "scalar_tensor_tensor", dst,
                                          srcp, mcol, dst, OP.mult, OP.add)
                            else:
                                if fg == 0:
                                    p.raw("vector", "tensor_copy", dst, srcp)
                                else:
                                    p.raw("vector", "tensor_tensor", dst, srcp,
                                          dst, OP.add)
                    p.bar("vector", "sV")
        for c in range(NCH):
            p.wait("sync", "sV", p.val("sV"))
            p.wait("sync", "sSY", p.val("sSY"))
            ld = p.dma("sync", sD, x_spill[c], "sSY")
            p.wait("vector", "sSY", ld)
            Yc = X[:, c, :]
            if is_l1:
                p.raw("vector", "tensor_tensor", sA, Yc, b2s[:], OP.add)
                p.drain("vector")
                p.raw("vector", "scalar_tensor_tensor", Yc, sA, 0.5, sD,
                      OP.mult, OP.add)
                p.bar("vector", "sV")
            else:
                p.raw("vector", "tensor_single_scalar", sB, Yc, 1.0, OP.is_gt)
                p.drain("vector")
                p.raw("vector", "tensor_copy", sC, sB)
                p.raw("vector", "tensor_copy", sA, Yc)
                p.drain("vector")
                for step in range(4):
                    p.raw("vector", "scalar_tensor_tensor", hr[:], sA[:, 0:512],
                          0.9, Yc[:, 0:512], OP.mult, OP.add)
                    p.raw("vector", "scalar_tensor_tensor", hq[:, 0:512],
                          sA[:, 512:1024], 0.9, Yc[:, 512:1024], OP.mult, OP.add)
                    p.drain("vector")
                    p.raw("vector", "tensor_tensor", sA[:, 0:512], hr[:],
                          sB[:, 0:512], OP.subtract)
                    p.raw("vector", "tensor_tensor", sA[:, 512:1024], hq[:, 0:512],
                          sB[:, 512:1024], OP.subtract)
                    p.drain("vector")
                    p.raw("vector", "tensor_single_scalar", sB, sA, 1.0, OP.is_gt)
                    p.drain("vector")
                    p.raw("vector", "tensor_tensor", sC, sC, sB, OP.add)
                    p.drain("vector")
                p.raw("vector", "scalar_tensor_tensor", Yc, sC, 0.1, sD,
                      OP.mult, OP.add)
                p.bar("vector", "sV")

    RG = [[0, 1], [2, 3], [4, 5], [6, 7]]

    def mixer_proj(wsrc, qscale=False):
        wh, wl = wsrc
        whv = WB16[:, 0:1024].rearrange("p (k n) -> p k n", n=128)
        wlv = WB16[:, 1024:2048].rearrange("p (k n) -> p k n", n=128)
        p.wait("sync", "sSY", p.val("sSY"))
        p.wait("sync", "sCC", p.val("sCC"))
        for t in range(3):
            for f in range(8):
                p.wait("sync", "sT", p.val("sT"))
                p.dma("sync", whv, wh[t, f].rearrange("k p n -> p k n"), "sWT")
                p.dma("sync", wlv, wl[t, f].rearrange("k p n -> p k n"), "sWT")
                p.wait("tensor", "sWT", p.val("sWT"))
                for tt in range(4):
                    ps = PS[tt % 2]
                    p.wait("tensor", "sV", p.val("sV"))
                    nmm = 0
                    for k in range(8):
                        for xa, wa in ((XTh, whv), (XTh, wlv), (XTl, whv)):
                            nmm += 1
                            p.raw("tensor", "matmul", ps, wa[:, k, :],
                                  xa[:, k, tt * 512:(tt + 1) * 512],
                                  start=(nmm == 1), stop=(nmm == 24))
                    tdd = p.bar("tensor", "sT")
                    p.wait("vector", "sT", tdd)
                    p.wait("vector", "sSY", p.val("sSY"))
                    slot = stage[:, (tt % 2) * 512:(tt % 2) * 512 + 512]
                    if qscale and t == 0:
                        p.raw("vector", "tensor_scalar_mul", slot, ps, 0.125)
                    else:
                        p.raw("vector", "tensor_copy", slot, ps)
                    vdd = p.bar("vector", "sV")
                    p.wait("sync", "sV", vdd)
                    dst = own_a[t, f] if f < 4 else cc_a_in[t, f - 4]
                    p.dma("sync", dst[:, tt * 512:(tt + 1) * 512], slot, "sSY")

    def exchange_a():
        p.wait("gpsimd", "sSY", p.val("sSY"))
        p.wait("gpsimd", "sCC", p.val("sCC"))

        def do_cc(e, sems):
            e.collective_compute("AllGather", OP.bypass, replica_groups=RG,
                                 ins=[cc_a_in], outs=[cc_a_out]) \
             .then_inc(sems["sCC"], 1)
        p.custom("gpsimd", do_cc, inc=(("sCC", 1),))
        p.wait("sync", "sCC", p.val("sCC"))
        p.wait("sync", "sSY", p.val("sSY"))

        def assemble(e, sems):
            for arm in nc.switch_core_id([nc.sync], n=128):
                half = int(arm.logical) % 2
                e.dma_start(out=kv1[:, :, :, half * TOK:(half + 1) * TOK],
                            in_=own_a).then_inc(sems["sSY"], 16)
                e.dma_start(out=kv1[:, :, :, (1 - half) * TOK:(2 - half) * TOK],
                            in_=cc_a_out[1 - half]).then_inc(sems["sSY"], 16)
        p.custom("sync", assemble, inc=(("sSY", 32),))
        return p.val("sSY")

    def exchange_b():
        p.wait("gpsimd", "sSY", p.val("sSY"))
        p.wait("gpsimd", "sCC", p.val("sCC"))
        p.wait("sync", "sSY", p.val("sSY"))
        p.wait("sync", "sCC", p.val("sCC"))

        def split(e, sems):
            for arm in nc.switch_core_id([nc.sync], n=128):
                half = int(arm.logical) % 2
                e.dma_start(out=cc_b_in,
                            in_=own_b[:, :, (1 - half) * TOK:(2 - half) * TOK]) \
                 .then_inc(sems["sSY"], 16)
                e.dma_start(out=ocanon[0:4].rearrange("q p t -> q p t"),
                            in_=own_b[:, :, half * TOK:(half + 1) * TOK]) \
                 .then_inc(sems["sSY"], 16)
        p.custom("sync", split, inc=(("sSY", 32),))
        p.wait("gpsimd", "sSY", p.val("sSY"))

        def do_cc(e, sems):
            e.collective_compute("AllGather", OP.bypass, replica_groups=RG,
                                 ins=[cc_b_in], outs=[cc_b_out]) \
             .then_inc(sems["sCC"], 1)
        p.custom("gpsimd", do_cc, inc=(("sCC", 1),))
        p.wait("sync", "sCC", p.val("sCC"))
        p.wait("sync", "sSY", p.val("sSY"))

        def gather(e, sems):
            for arm in nc.switch_core_id([nc.sync], n=128):
                half = int(arm.logical) % 2
                e.dma_start(out=ocanon[4:8], in_=cc_b_out[1 - half]) \
                 .then_inc(sems["sSY"], 16)
        p.custom("sync", gather, inc=(("sSY", 16),))
        return p.val("sSY")

    def wo_proj(wsrc, scale_half):
        wh, wl = wsrc
        p.wait("sync", "sT", p.val("sT"))
        p.wait("sync", "sV", p.val("sV"))
        p.wait("sync", "sSY", p.val("sSY"))
        whv = WB16[:, 0:8192].rearrange("p (j d) -> p j d", d=1024)
        wlv = WB16[:, 8192:16384].rearrange("p (j d) -> p j d", d=1024)
        ldw = p.dma("sync", whv, wh.rearrange("j p d -> p j d"), "sWT")
        ldw = p.dma("sync", wlv, wl.rearrange("j p d -> p j d"), "sWT")
        for j in range(8):
            lo = p.dma("sync", stage[:], ocanon[j, :, 0:1024], "sSY")
            lo2 = p.dma("sync", attnb[:], ocanon[j, :, 1024:1536], "sSY")
            lo3 = p.dma("sync", hr[:], ocanon[j, :, 1536:2048], "sSY")
            p.wait("vector", "sSY", lo3)
            p.raw("vector", "tensor_copy", XTh[:, j, 0:1024], stage[:])
            p.raw("vector", "tensor_copy", XTh[:, j, 1024:1536], attnb[:])
            p.raw("vector", "tensor_copy", XTh[:, j, 1536:2048], hr[:])
            p.drain("vector")
            p.raw("vector", "scalar_tensor_tensor", XTl[:, j, 0:1024], stage[:],
                  1.0, XTh[:, j, 0:1024], OP.mult, OP.subtract)
            p.raw("vector", "scalar_tensor_tensor", XTl[:, j, 1024:1536], attnb[:],
                  1.0, XTh[:, j, 1024:1536], OP.mult, OP.subtract)
            p.raw("vector", "scalar_tensor_tensor", XTl[:, j, 1536:2048], hr[:],
                  1.0, XTh[:, j, 1536:2048], OP.mult, OP.subtract)
            vs = p.bar("vector", "sV")
            p.wait("sync", "sV", vs)
        vsp = p.val("sV")
        p.wait("tensor", "sV", vsp)
        p.wait("tensor", "sWT", p.val("sWT"))
        for c in range(NCH):
            p.wait("tensor", "sV", p.val("sV"))
            for half in range(2):
                nmm = 0
                for j in range(8):
                    for oa, wa in ((XTh, whv), (XTh, wlv), (XTl, whv)):
                        nmm += 1
                        p.raw("tensor", "matmul", PS[half],
                              oa[:, j, c * 128:(c + 1) * 128],
                              wa[:, j, half * 512:(half + 1) * 512],
                              start=(nmm == 1), stop=(nmm == 24))
            tdd = p.bar("tensor", "sT")
            p.wait("vector", "sT", tdd)
            for half in range(2):
                dst = X[:, c, half * 512:(half + 1) * 512]
                p.raw("vector", "tensor_tensor", dst, PS[half], dst, OP.add)
            p.bar("vector", "sV")

    qkv_view = WBUF[:, 0:6144].rearrange("p (q r t) -> p q r t", q=3, r=NPAIR)
    SL = lambda i: BIG[0:64, 2 + i // 4, (i % 4) * 512:(i % 4) * 512 + 512]
    S8 = lambda a: stats[0:64, a:a + 8]
    V8 = lambda ap: ap.rearrange("t (s d) -> t s d", d=64)
    odsb_b = BIG[:, 6, 0:256]
    bstf_f = bstf.rearrange("t q d -> t (q d)")
    tgf_f = tgf.rearrange("t q d -> t (q d)")
    tbef_f = tbef.rearrange("t q d -> t (q d)")
    Wst_f = Wst.rearrange("p q d -> p (q d)")

    def ttt_chunk(c8):
        c64 = slice(c8 * 64, (c8 + 1) * 64)
        xkt, xvt, z1, sq, xhat, B2, go, gxh, t4, gx, REP1, REP2 = \
            [SL(i) for i in range(12)]
        pr = sq

        p.wait("tensor", "sV", p.val("sV"))
        p.wait("tensor", "sS", p.val("sS"))
        for pair in range(4):
            p.raw("tensor", "transpose", PS[1][0:64, pair * 128:(pair + 1) * 128],
                  qkv_view[:, 1, pair, c64], ident[:])
        for pair in range(4):
            p.raw("tensor", "transpose", PS[2][0:64, pair * 128:(pair + 1) * 128],
                  qkv_view[:, 2, pair, c64], ident[:])
        for hh in range(2):
            p.drain("tensor")
            hs = slice(64 * hh, 64 * hh + 64)
            for pair in range(4):
                p.raw("tensor", "matmul", PS[6][0:64, 2 * pair + hh:2 * pair + hh + 1],
                      qkv_view[hs, 1, pair, c64], tth[hs, pair:pair + 1],
                      start=True, stop=True)
                p.raw("tensor", "matmul",
                      PS[0][0:64, pair * 128 + hh * 64:pair * 128 + hh * 64 + 64],
                      qkv_view[hs, 1, pair, c64], Wst[hs, pair, :],
                      start=True, stop=True)
        tA = p.bar("tensor", "sT")

        p.wait("scalar", "sT", tA)
        p.raw("scalar", "copy", xkt, PS[1][0:64, 0:512])
        p.drain("scalar")
        p.raw("scalar", "copy", xvt, PS[2][0:64, 0:512])
        sE = p.bar("scalar", "sS")
        p.wait("vector", "sT", tA)
        p.raw("vector", "tensor_tensor", tcol[0:64, 16:24], PS[6][0:64, 0:8],
              ttbf[:], OP.add)
        p.drain("vector")
        p.raw("vector", "tensor_tensor", z1, PS[0][0:64, 0:512], bstf_f, OP.add)
        v1 = p.bar("vector", "sV")
        p.wait("scalar", "sV", v1)
        p.raw("scalar", "activation", tcol[0:64, 24:32], tcol[0:64, 16:24], AF.Sigmoid)
        p.drain("scalar")
        p.raw("scalar", "activation", sq, z1, AF.Square)
        s1_ = p.bar("scalar", "sS")
        p.wait("vector", "sS", sE)
        p.raw("vector", "tensor_tensor", B2, xkt, xvt, OP.subtract)
        p.drain("vector")
        p.raw("vector", "reduce_sum", S8(0), V8(z1), axis=AX)
        p.raw("vector", "tensor_tensor", B2, B2, tbef_f, OP.add)
        p.wait("vector", "sS", s1_)
        p.drain("vector")
        p.raw("vector", "reduce_sum", S8(8), V8(sq), axis=AX)
        p.raw("vector", "tensor_scalar_mul", S8(16), S8(0), 1.0 / 64)
        p.drain("vector")
        p.raw("vector", "tensor_scalar_mul", S8(24), S8(8), 1.0 / 64)
        p.raw("vector", "tensor_tensor", S8(32), S8(16), S8(16), OP.mult)
        p.drain("vector")
        p.raw("vector", "tensor_tensor", S8(40), S8(32), S8(24), OP.subtract)
        v2 = p.bar("vector", "sV")
        p.wait("scalar", "sV", v2)
        p.raw("scalar", "activation", S8(48), S8(40), AF.Sqrt, bias=EPS_TTT,
              scale=-1.0)
        s2_ = p.bar("scalar", "sS")
        p.wait("vector", "sS", s2_)
        p.raw("vector", "reciprocal", S8(56), S8(48))
        p.raw("vector", "scalar_tensor_tensor", tcol[0:64, 16:24],
              tcol[0:64, 24:32], 1.0 / 64, eta0.broadcast_to((64, 8)),
              OP.mult, OP.mult)
        p.drain("vector")
        p.raw("vector", "tensor_copy", REP1,
              S8(16).unsqueeze(2).broadcast_to((64, 8, 64)))
        p.raw("vector", "tensor_copy", REP2,
              S8(56).unsqueeze(2).broadcast_to((64, 8, 64)))
        p.drain("vector")
        p.raw("vector", "tensor_tensor", xhat, z1, REP1, OP.subtract)
        p.drain("vector")
        p.raw("vector", "tensor_tensor", xhat, xhat, REP2, OP.mult)
        p.drain("vector")
        p.raw("vector", "tensor_tensor", go, xhat, tgf_f, OP.mult)
        p.drain("vector")
        p.raw("vector", "tensor_tensor", go, go, B2, OP.add)
        p.drain("vector")
        p.raw("vector", "tensor_tensor", gxh, go, tgf_f, OP.mult)
        p.drain("vector")
        p.raw("vector", "reduce_sum", S8(0), V8(gxh), axis=AX, negate=True)
        p.raw("vector", "tensor_tensor", pr, gxh, xhat, OP.mult)
        p.drain("vector")
        p.raw("vector", "reduce_sum", S8(8), V8(pr), axis=AX, negate=True)
        p.raw("vector", "tensor_scalar_mul", t4, gxh, 64.0)
        p.drain("vector")
        p.raw("vector", "tensor_copy", REP1,
              S8(0).unsqueeze(2).broadcast_to((64, 8, 64)))
        p.raw("vector", "tensor_copy", REP2,
              S8(8).unsqueeze(2).broadcast_to((64, 8, 64)))
        p.drain("vector")
        p.raw("vector", "tensor_tensor", t4, t4, REP1, OP.add)
        p.raw("vector", "tensor_tensor", gx, xhat, REP2, OP.mult)
        p.drain("vector")
        p.raw("vector", "tensor_tensor", gx, gx, t4, OP.add)
        p.raw("vector", "tensor_scalar_mul", S8(48), S8(56), 1.0 / 64)
        p.drain("vector")
        p.raw("vector", "tensor_copy", REP1,
              S8(48).unsqueeze(2).broadcast_to((64, 8, 64)))
        p.raw("vector", "tensor_copy", REP2,
              tcol[0:64, 16:24].unsqueeze(2).broadcast_to((64, 8, 64)))
        p.drain("vector")
        p.raw("vector", "tensor_tensor", gx, gx, REP1, OP.mult)
        p.raw("vector", "tensor_tensor", xkt, xkt, REP2, OP.mult)
        vA = p.bar("vector", "sV")

        p.wait("tensor", "sV", vA)
        for pair in range(4):
            for hh in range(2):
                seg = slice((2 * pair + hh) * 64, (2 * pair + hh) * 64 + 64)
                p.raw("tensor", "matmul",
                      PS[4][64 * hh:64 * hh + 64, pair * 64:(pair + 1) * 64],
                      xkt[:, seg], gx[:, seg], start=True, stop=True)
                p.raw("tensor", "matmul", PS[5][0:64, seg],
                      REP2[:, seg], gx[:, seg], start=True, stop=True)
        tB = p.bar("tensor", "sT")
        p.wait("vector", "sT", tB)
        p.raw("vector", "tensor_tensor", Wst_f, Wst_f, PS[4][:, 0:256], OP.subtract)
        p.raw("vector", "tensor_tensor", bstf_f, bstf_f, PS[5][0:64, 0:512],
              OP.subtract)
        vB = p.bar("vector", "sV")

        p.wait("tensor", "sV", vB)
        for hh in range(2):
            if hh:
                p.drain("tensor")
            hs = slice(64 * hh, 64 * hh + 64)
            for pair in range(4):
                p.raw("tensor", "matmul",
                      PS[3][0:64, pair * 128 + hh * 64:pair * 128 + hh * 64 + 64],
                      qkv_view[hs, 0, pair, c64], Wst[hs, pair, :],
                      start=True, stop=True)
        tC = p.bar("tensor", "sT")

        p.wait("vector", "sT", tC)
        p.raw("vector", "tensor_tensor", z1, PS[3][0:64, 0:512], bstf_f, OP.add)
        v3 = p.bar("vector", "sV")
        p.wait("scalar", "sV", v3)
        p.raw("scalar", "activation", sq, z1, AF.Square)
        s3_ = p.bar("scalar", "sS")
        p.wait("vector", "sS", s3_)
        p.raw("vector", "reduce_sum", S8(0), V8(z1), axis=AX)
        p.raw("vector", "reduce_sum", S8(8), V8(sq), axis=AX)
        p.drain("vector")
        p.raw("vector", "tensor_scalar_mul", S8(16), S8(0), 1.0 / 64)
        p.raw("vector", "tensor_scalar_mul", S8(24), S8(8), 1.0 / 64)
        p.drain("vector")
        p.raw("vector", "tensor_tensor", S8(32), S8(16), S8(16), OP.mult)
        p.drain("vector")
        p.raw("vector", "tensor_tensor", S8(40), S8(32), S8(24), OP.subtract)
        v4 = p.bar("vector", "sV")
        p.wait("scalar", "sV", v4)
        p.raw("scalar", "activation", S8(48), S8(40), AF.Sqrt, bias=EPS_TTT,
              scale=-1.0)
        s4_ = p.bar("scalar", "sS")
        p.wait("vector", "sS", s4_)
        p.raw("vector", "reciprocal", S8(56), S8(48))
        p.drain("vector")
        p.raw("vector", "tensor_copy", REP1,
              S8(16).unsqueeze(2).broadcast_to((64, 8, 64)))
        p.raw("vector", "tensor_copy", REP2,
              S8(56).unsqueeze(2).broadcast_to((64, 8, 64)))
        p.drain("vector")
        p.raw("vector", "tensor_tensor", xhat, z1, REP1, OP.subtract)
        p.drain("vector")
        p.raw("vector", "tensor_tensor", xhat, xhat, REP2, OP.mult)
        p.drain("vector")
        p.raw("vector", "tensor_tensor", go, xhat, tgf_f, OP.mult)
        p.drain("vector")
        p.raw("vector", "tensor_tensor", go, go, tbef_f, OP.add)
        vC = p.bar("vector", "sV")

        p.wait("tensor", "sV", vC)
        for pair in range(4):
            p.raw("tensor", "transpose", PS[7][:, pair * 64:(pair + 1) * 64],
                  go[:, pair * 128:(pair + 1) * 128], ident[0:64, 0:64])
        tD = p.bar("tensor", "sT")
        p.wait("vector", "sT", tD)
        p.wait("vector", "sSY", p.val("sSY"))
        p.raw("vector", "tensor_tensor", odsb_b, PS[7][:, 0:256],
              qkv_view[:, 0, :, c64], OP.add)
        return p.bar("vector", "sV")

    def ttt_sublayer():
        ln_to_xt()
        mixer_proj((wqkvT_h, wqkvT_l))
        if tstage < 2:
            return
        exchange_a()
        if tstage < 3:
            return
        for g in range(8):
            p.wait("sync", "sT", p.val("sT"))
            p.wait("sync", "sV", p.val("sV"))
            p.wait("sync", "sSY", p.val("sSY"))
            ld = p.dma("sync", qkv_view,
                       kv1[:, :, :, g * 512:(g + 1) * 512]
                       .rearrange("q r p t -> p q r t"), "sSY")
            p.wait("tensor", "sSY", ld)
            p.wait("vector", "sSY", ld)
            for c8 in range(8):
                if titer is not None and g * 8 + c8 >= titer:
                    continue
                vD = ttt_chunk(c8)
                p.wait("sync", "sV", vD)
                c = g * 8 + c8
                p.dma("sync", own_b[:, :, c * 64:(c + 1) * 64]
                      .rearrange("q p t -> p q t"),
                      odsb_b.rearrange("p (r t) -> p r t", t=64), "sSY")
        if tstage < 4:
            return
        exchange_b()
        if tstage < 5:
            return
        wo_proj((wo_r_h, wo_r_l), False)

    RSQRT2 = 0.7071067811865476

    def attn_sublayer():
        ln_to_xt()
        mixer_proj((awqkvT_h, awqkvT_l), qscale=True)
        exchange_a()
        khi = WB16[:, 0:4096]
        klo = WB16[:, 4096:8192]
        qhi = WB16[:, 8192:12288]
        qlo = WB16[:, 12288:16384]
        vtk_h = BIG16[:, 0:2, :].rearrange("p a t -> p (a t)")[:, 0:4160] \
            .rearrange("p (c t) -> p c t", t=130)
        vtk_l = BIG16[:, 2:4, :].rearrange("p a t -> p (a t)")[:, 0:4160] \
            .rearrange("p (c t) -> p c t", t=130)
        fstg = BIG[:, 4:6, :].rearrange("p a t -> p (a t)")
        hq16 = hq.bitcast(BF16)
        p.wait("vector", "sT", p.val("sT"))
        p.wait("vector", "sV", p.val("sV"))
        p.wait("vector", "sS", p.val("sS"))
        triu16 = maskb.bitcast(BF16).rearrange("p a b -> p (a b)")[:, 0:128]
        onesr = stats[0:1, 64:128]
        p.raw("vector", "tensor_copy", triu16, triu[:])
        p.raw("vector", "memset", vtk_h[:, :, 64:65], 1.0)
        p.raw("vector", "memset", vtk_h[:, :, 129:130], 1.0)
        p.raw("vector", "memset", vtk_l[:, :, 64:65], 0.0)
        p.raw("vector", "memset", vtk_l[:, :, 129:130], 0.0)
        p.raw("vector", "memset", onesr, 1.0)
        p.bar("vector", "sV")
        for pair in range(NPAIR):
            p.wait("sync", "sT", p.val("sT"))
            p.wait("sync", "sV", p.val("sV"))
            p.wait("sync", "sSY", p.val("sSY"))
            ldk = p.dma("sync", fstg, kv1[1, pair], "sSY")
            p.wait("vector", "sSY", ldk)
            p.raw("vector", "tensor_copy", khi, fstg)
            p.drain("vector")
            p.raw("vector", "scalar_tensor_tensor", klo, fstg, 1.0, khi,
                  OP.mult, OP.subtract)
            vk = p.bar("vector", "sV")
            p.wait("sync", "sV", vk)
            ldq = p.dma("sync", fstg, kv1[0, pair], "sSY")
            p.wait("vector", "sSY", ldq)
            p.raw("vector", "tensor_copy", qhi, fstg)
            p.drain("vector")
            p.raw("vector", "scalar_tensor_tensor", qlo, fstg, 1.0, qhi,
                  OP.mult, OP.subtract)
            vq = p.bar("vector", "sV")
            p.wait("sync", "sV", vq)
            ldv = p.dma("sync", fstg, kv1[2, pair], "sSY")
            p.wait("tensor", "sSY", ldv)
            for kb in range(32):
                ps = PS[7][:, (kb % 2) * 256:(kb % 2) * 256 + 128]
                p.wait("tensor", "sV", p.val("sV"))
                p.raw("tensor", "transpose", ps, fstg[:, kb * 128:(kb + 1) * 128],
                      ident[:])
                td = p.bar("tensor", "sT")
                p.wait("vector", "sT", td)
                p.raw("vector", "tensor_copy", vtk_h[:, kb, 0:64], ps[:, 0:64])
                p.raw("vector", "tensor_copy", vtk_h[:, kb, 65:129], ps[:, 64:128])
                p.drain("vector")
                p.raw("vector", "scalar_tensor_tensor", vtk_l[:, kb, 0:64],
                      ps[:, 0:64], 1.0, vtk_h[:, kb, 0:64], OP.mult, OP.subtract)
                p.raw("vector", "scalar_tensor_tensor", vtk_l[:, kb, 65:129],
                      ps[:, 64:128], 1.0, vtk_h[:, kb, 65:129], OP.mult, OP.subtract)
                p.bar("vector", "sV")
            fsl4 = [attnb[:], hr[:], stage[:, 0:512], stage[:, 512:1024]]
            ahi4 = [BIG16[:, 6 + i // 2,
                          (i % 2) * 2048:(i % 2) * 2048 + 512] for i in range(4)]
            alo4 = [BIG16[:, 6 + i // 2,
                          (i % 2) * 2048 + 512:(i % 2) * 2048 + 1024] for i in range(4)]
            rz1 = xn[0:1, 0, 0:512]
            zb = xn[0:1, 0, 512:1024]
            rzs2 = (hq[0:64, 0:512], hq[0:64, 512:1024])
            onorm2 = (fstg[0:64, 0:512], fstg[0:64, 512:1024])
            sq_done = {}
            ahi_inc = {}
            ready = {}
            ready_av = {}
            ti_inc = {}
            dma_inc = {}
            rvi_d = {}
            s0s = p.val("sS")
            s0t = p.val("sT")
            s0v = p.val("sV")
            s0sy = p.bar("sync", "sSY")
            p.wait("tensor", "sV", s0v)
            p.wait("tensor", "sS", s0s)
            pend = [None]
            g_base = [0]
            r_idx = [0]

            def run_round(qt, hh):
                hb = 64 * hh
                r = r_idx[0]
                g0 = g_base[0]
                nkb = (qt + 1) * 4
                ob = PS[0] if r % 2 == 0 else PS[2]
                rb_ps = PS[1] if r % 2 == 0 else PS[3]

                def sqd(i):
                    return sq_done[i] if i >= 0 else s0s

                def rdy(i):
                    return ready[i] if i >= 0 else s0v

                def rav(i):
                    return ready_av[i] if i >= 0 else s0t

                def emit_scores(kb):
                    gi = g0 + kb
                    sl = gi % 4
                    ps_s = PS[4 + sl]
                    p.wait("tensor", "sS", sqd(gi - 4))
                    nmm = 0
                    for ka, qa in ((khi, qhi), (khi, qlo), (klo, qhi)):
                        nmm += 1
                        p.raw("tensor", "matmul", ps_s,
                              ka[hb:hb + 64, kb * 128:(kb + 1) * 128],
                              qa[hb:hb + 64, qt * 512:(qt + 1) * 512],
                              start=(nmm == 1), stop=(nmm == 3),
                              inc=((("sT", 1),) if nmm == 3 else ()))
                    p.wait("scalar", "sT", p.val("sT"))
                    p.wait("scalar", "sV", rdy(gi - 4))
                    p.raw("scalar", "activation", fsl4[sl], ps_s, AF.Square,
                          bias=RSQRT2, scale=RSQRT2, inc=(("sS", 1),))
                    sq_done[gi] = p.val("sS")

                def emit_split(kb):
                    gi = g0 + kb
                    sl = gi % 4
                    p.wait("vector", "sT", rav(gi - 4))
                    p.wait("vector", "sS", sq_done[gi])
                    p.raw("vector", "tensor_scalar_add", ahi4[sl], fsl4[sl],
                          0.5, inc=(("sV", 1),))
                    ahi_inc[gi] = p.val("sV")

                def emit_lo(kb):
                    gi = g0 + kb
                    sl = gi % 4
                    p.wait("vector", "sT", rav(gi - 4))
                    p.wait("vector", "sV", ahi_inc[gi])
                    p.raw("vector", "scalar_tensor_tensor", alo4[sl],
                          fsl4[sl], 0.5, ahi4[sl], OP.add, OP.subtract,
                          inc=(("sV", 1),))
                    ready[gi] = p.val("sV")
                    if kb >= qt * 4:
                        ds = kb - qt * 4
                        p.drain("vector")
                        p.raw("vector", "tensor_tensor",
                              ahi4[sl][:, ds * 128:(ds + 1) * 128],
                              ahi4[sl][:, ds * 128:(ds + 1) * 128], triu16,
                              OP.mult)
                        p.raw("vector", "tensor_tensor",
                              alo4[sl][:, ds * 128:(ds + 1) * 128],
                              alo4[sl][:, ds * 128:(ds + 1) * 128], triu16,
                              OP.mult)
                        if ds:
                            p.raw("vector", "memset",
                                  ahi4[sl][:, 0:ds * 128], 0.0)
                            p.raw("vector", "memset",
                                  alo4[sl][:, 0:ds * 128], 0.0)
                        p.drain("vector", inc=(("sV", 1),))
                        ready[gi] = p.val("sV")

                def emit_av(kb):
                    gi = g0 + kb
                    sl = gi % 4
                    p.wait("tensor", "sV", ready[gi])
                    for ip, (aa, vv) in enumerate(((ahi4[sl], vtk_h),
                                                   (ahi4[sl], vtk_l),
                                                   (alo4[sl], vtk_h))):
                        p.raw("tensor", "matmul", ob[0:65, 0:512],
                              vv[:, kb, 65 * hh:65 * hh + 65], aa,
                              start=(kb == 0 and ip == 0),
                              stop=(kb == nkb - 1 and ip == 2),
                              inc=((("sT", 1),) if ip == 2 else ()))
                    ready_av[gi] = p.val("sT")

                LAG = 3
                for kb in range(nkb):
                    emit_scores(kb)
                    if kb == 1 and pend[0] is not None:
                        pend[0]()
                        pend[0] = None
                    if kb >= LAG:
                        emit_lo(kb - LAG)
                        emit_av(kb - LAG)
                    emit_split(kb)
                for kb in range(max(0, nkb - LAG), nkb):
                    emit_lo(kb)
                    emit_av(kb)
                p.wait("scalar", "sT", ready_av[g0 + nkb - 1])
                if r - 1 in rvi_d:
                    p.wait("scalar", "sV", rvi_d[r - 1])
                p.raw("scalar", "activation", zb, ob[64:65, 0:512],
                      AF.Copy, bias=EPS_ATT, inc=(("sS", 1),))
                zi = p.val("sS")
                p.wait("vector", "sS", zi)
                p.raw("vector", "reciprocal", rz1, zb, inc=(("sV", 1),))
                rvi_d[r] = p.val("sV")
                p.wait("tensor", "sV", rvi_d[r])
                p.raw("tensor", "matmul", rb_ps[0:64, 0:512], onesr, rz1,
                      start=True, stop=True, inc=(("sT", 1),))
                rmi = p.val("sT")

                def pendfn(r=r, ob=ob, rb_ps=rb_ps, rmi=rmi, qt=qt, hh=hh,
                           pair=pair):
                    p.wait("scalar", "sT", rmi)
                    if r - 2 in ti_inc:
                        p.wait("scalar", "sV", ti_inc[r - 2])
                    p.raw("scalar", "activation", rzs2[r % 2],
                          rb_ps[0:64, 0:512], AF.Copy, inc=(("sS", 1),))
                    ci = p.val("sS")
                    p.wait("vector", "sS", ci)
                    p.wait("vector", "sSY", dma_inc.get(r - 2, s0sy))
                    p.raw("vector", "tensor_tensor", onorm2[r % 2],
                          ob[0:64, 0:512], rzs2[r % 2], OP.mult,
                          inc=(("sV", 1),))
                    ti_inc[r] = p.val("sV")
                    p.wait("sync", "sV", ti_inc[r])
                    p.dma("sync",
                          own_b[pair][hh * 64:(hh + 1) * 64,
                                      qt * 512:(qt + 1) * 512],
                          onorm2[r % 2], "sSY")
                    dma_inc[r] = p.bar("sync", "sSY")

                pend[0] = pendfn
                r_idx[0] += 1
                g_base[0] += nkb

            for qt in range(8):
                for hh in range(2):
                    run_round(qt, hh)
            pend[0]()
            pend[0] = None
        exchange_b()
        wo_proj((awoT_h, awoT_l), False)

    ln_to_xt()
    ffn("L0")
    if upto >= 2:
        ttt_sublayer()
    if upto >= 3:
        ln_to_xt()
        ffn("L0")
    if upto >= 4:
        ln_to_xt()
        ffn("L1")
    if upto >= 5:
        attn_sublayer()
    if upto >= 6:
        ln_to_xt()
        ffn("L1")

    vfin = p.bar("vector", "sV")
    p.wait("sync", "sV", vfin)
    p.dma("sync", y_out.rearrange("c p d -> p c d"), X[:], "sOUT")
    p.wait("sync", "sOUT", p.val("sOUT"))

    import contextlib
    S = {}
    with contextlib.ExitStack() as st:
        for sn in semnames:
            S[sn] = st.enter_context(nc.semaphore(sn))
        block = st.enter_context(nc.Block())

        @block.sync
        def _(e):
            _exec_ops(e, p.ops["sync"], S)

        @block.tensor
        def _(e):
            _exec_ops(e, p.ops["tensor"], S)

        @block.vector
        def _(e):
            _exec_ops(e, p.ops["vector"], S)

        @block.scalar
        def _(e):
            _exec_ops(e, p.ops["scalar"], S)

        @block.gpsimd
        def _(e):
            _exec_ops(e, p.ops["gpsimd"], S)

    return nc


def prep_inputs(inp):
    x = np.asarray(inp["L0_attn_hidden_states"], np.float32)

    def t32(a):
        return np.ascontiguousarray(np.asarray(a, np.float32))

    def fc_tiles(wT):
        Dd, Ff = wT.shape
        return np.ascontiguousarray(
            wT.reshape(Dd // 128, 128, Ff // 128, 128).transpose(2, 0, 1, 3))

    wfcT = fc_tiles(t32(inp["L0_ffn_W_fc"]).T)
    wprojT = np.ascontiguousarray(t32(inp["L0_ffn_W_proj"]).T.reshape(32, 128, D))
    w1T = fc_tiles(t32(inp["L1_ffn_w1"]).T)
    w2T = np.ascontiguousarray(t32(inp["L1_ffn_w2"]).T.reshape(32, 128, D))
    routerT = np.ascontiguousarray(t32(inp["L1_ffn_router_w"]).T.reshape(8, 128, 16))
    rb_rep = np.ascontiguousarray(
        np.broadcast_to(t32(inp["L1_ffn_router_b"])[None, :], (128, 16)))
    b1cols = np.ascontiguousarray(t32(inp["L1_ffn_b1"]).reshape(32, 128).T)
    b2rep = np.ascontiguousarray(np.broadcast_to(t32(inp["L1_ffn_b2"])[None, :], (128, D)))

    wq, wk, wv = t32(inp["L0_attn_wq"]), t32(inp["L0_attn_wk"]), t32(inp["L0_attn_wv"])
    wo = t32(inp["L0_attn_wo"])
    aWq, aWk, aWv = t32(inp["L1_attn_Wq"]), t32(inp["L1_attn_Wk"]), t32(inp["L1_attn_Wv"])
    aWo = t32(inp["L1_attn_Wo"])

    ident = np.eye(128, dtype=np.float32)
    triu = np.triu(np.ones((128, 128), np.float32))

    common = dict(wfcT=wfcT, wprojT=wprojT, w1T=w1T, w2T=w2T, routerT=routerT,
                  rb_rep=rb_rep, b1cols=b1cols, b2rep=b2rep, ident=ident, triu=triu)

    W1 = t32(inp["L0_attn_W1"])
    b1t = t32(inp["L0_attn_b1"]).reshape(16, 64)
    theta = t32(inp["L0_attn_theta"]).reshape(16, 64, 1)
    tb = t32(inp["L0_attn_theta_bias"]).reshape(16)
    alpha = t32(inp["L0_attn_alpha"])
    gamma = t32(inp["L0_attn_gamma"]).reshape(16, 64)
    beta = t32(inp["L0_attn_beta"]).reshape(16, 64)
    invk = (1.0 / np.arange(1, KCH + 1)).astype(np.float32)

    in_maps = []
    for c in range(NCORE):
        b, half = divmod(c, 2)
        hsel = list(range(8)) if half == 0 else list(range(8, 16))
        osel = [h for h in range(16) if h not in hsel]
        order = hsel + osel

        def headperm_cols(w):
            wr = w.reshape(w.shape[0], 16, 64)[:, order, :].reshape(w.shape[0], 1024)
            return fc_tiles(np.ascontiguousarray(wr))

        wqkvT = np.stack([headperm_cols(wq), headperm_cols(wk), headperm_cols(wv)])
        awqkvT = np.stack([headperm_cols(aWq.T), headperm_cols(aWk.T),
                           headperm_cols(aWv.T)])
        wo_r = np.ascontiguousarray(wo.reshape(16, 64, D)[order].reshape(8, 128, D))
        awoT = np.ascontiguousarray(aWo.T.reshape(16, 64, D)[order].reshape(8, 128, D))

        def pairs(a, rep=False):
            outs = []
            for pp in range(4):
                h0, h1 = hsel[2 * pp], hsel[2 * pp + 1]
                x0 = np.asarray(a[h0]); x1 = np.asarray(a[h1])
                if rep:
                    x0 = np.broadcast_to(x0.reshape(1, -1), (64, x0.size))
                    x1 = np.broadcast_to(x1.reshape(1, -1), (64, x1.size))
                if x0.ndim == 1:
                    x0 = x0.reshape(64, 1); x1 = x1.reshape(64, 1)
                outs.append(np.concatenate([x0, x1], 0))
            return np.ascontiguousarray(np.stack(outs)).astype(np.float32)

        ttt_W0 = pairs(W1)
        ttt_th = np.ascontiguousarray(pairs(theta).reshape(4, 128).T)

        def tmrows(a):
            out = np.zeros((64, 4, 128), np.float32)
            for pp in range(4):
                h0, h1 = hsel[2 * pp], hsel[2 * pp + 1]
                out[:, pp, 0:64] = a[h0][None, :]
                out[:, pp, 64:128] = a[h1][None, :]
            return np.ascontiguousarray(out)

        ttt_b0f = tmrows(b1t)
        ttt_gf = tmrows(gamma)
        ttt_bef = tmrows(beta)
        ttt_tbf = np.zeros((64, 8), np.float32)
        for pp in range(4):
            ttt_tbf[:, 2 * pp] = tb[hsel[2 * pp]]
            ttt_tbf[:, 2 * pp + 1] = tb[hsel[2 * pp + 1]]
        ttt_al = np.ascontiguousarray(alpha.reshape(64, 1))
        ttt_ik = np.ascontiguousarray(invk.reshape(64, 1))

        x_core = np.ascontiguousarray(
            x[b, half * TOK:(half + 1) * TOK].reshape(NCH, 128, D))

        m = dict(common)
        m["cvals"] = np.ascontiguousarray(np.broadcast_to(
            np.array([1e-5, 1e-6, -1.0, 1.0, 0.0, 0.5, 0.1, 0.9,
                      0.7071067811865476, 0.0, 0.0, 0.0], np.float32)[None, :],
            (128, 12)))
        m.update(x_in=x_core, wqkvT=wqkvT, wo_r=wo_r, awqkvT=awqkvT, awoT=awoT,
                 ttt_W0=ttt_W0, ttt_b0f=ttt_b0f, ttt_th=ttt_th, ttt_tbf=ttt_tbf,
                 ttt_gf=ttt_gf, ttt_bef=ttt_bef, ttt_al=ttt_al, ttt_ik=ttt_ik)
        in_maps.append(m)
    return in_maps


_CACHE = {}


def kernel(**inputs):
    in_maps = prep_inputs(inputs)
    key = (_UPTO, _TSTAGE, _TITER, _TPHASE, _TSEL)
    if key not in _CACHE:
        _CACHE[key] = build_program(_UPTO, _TSTAGE, _TITER, _TPHASE, _TSEL)
    nc = _CACHE[key]
    res = run_bass_kernel_spmd(nc, in_maps, core_ids=list(range(NCORE)))
    out = np.empty((4, 4096, D), np.float32)
    for c in range(NCORE):
        b, half = divmod(c, 2)
        out[b, half * TOK:(half + 1) * TOK] = \
            np.asarray(res.results[c]["y"]).reshape(TOK, D)
    return out


# version 12
# speedup vs baseline: 1.0878x; 1.0878x over previous
import numpy as np
import concourse.bass as bass
import concourse.mybir as mybir
from concourse.bass_utils import run_bass_kernel_spmd
from concourse.libnrt import node_relative_to_routing_address

F32 = mybir.dt.float32
BF16 = mybir.dt.bfloat16
OP = mybir.AluOpType
AF = mybir.ActivationFunctionType
AX = mybir.AxisListType.X

NCORE = 8
TOK = 2048
NCH = 16
D = 1024
FF = 4096
HD = 64
NPAIR = 4
KCH = 64
NKCH = 64
EPS_LN = 1e-5
EPS_TTT = 1e-6
EPS_ATT = 1e-6

_UPTO = 6
_TSTAGE = 5
_TITER = None
_TPHASE = 9
_TSEL = "all"

_ENGINES = ("sync", "tensor", "vector", "scalar", "gpsimd")


class Plan:
    def __init__(self):
        self.ops = {e: [] for e in _ENGINES}
        self.cnt = {}

    def raw(self, eng, fname, *args, inc=(), **kw):
        for s, a in inc:
            self.cnt[s] = self.cnt.get(s, 0) + a
        self.ops[eng].append((fname, args, kw, tuple(inc)))

    def wait(self, eng, sem, val):
        self.ops[eng].append(("__wait__", (sem, val), {}, ()))

    def drain(self, eng, inc=()):
        self.raw(eng, "__drain__", inc=inc)

    def bar(self, eng, sem):
        self.drain(eng, inc=((sem, 1),))
        return self.cnt[sem]

    def dma(self, eng, out, in_, sem):
        self.raw(eng, "dma_start", inc=((sem, 16),), out=out, in_=in_)
        return self.cnt[sem]

    def custom(self, eng, fn, inc=()):
        for s, a in inc:
            self.cnt[s] = self.cnt.get(s, 0) + a
        self.ops[eng].append(("__custom__", (fn,), {}, tuple(inc)))

    def val(self, sem):
        return self.cnt.get(sem, 0)


def _exec_ops(eng_obj, oplist, sems):
    for fn, args, kw, inc in oplist:
        if fn == "__wait__":
            eng_obj.wait_ge(sems[args[0]], args[1])
            continue
        if fn == "__drain__":
            inst = eng_obj.drain()
        elif fn == "__custom__":
            args[0](eng_obj, sems)
            inst = None
        else:
            inst = getattr(eng_obj, fn)(*args, **kw)
        if inst is not None:
            for s, a in inc:
                inst.then_inc(sems[s], a)


def build_program(upto, tstage=5, titer=None, tphase=9, tsel='all'):
    nc = bass.Bass("TRN2", target_bir_lowering=False)

    def In(name, shape, dtype=F32):
        return nc.dram_tensor(name, list(shape), dtype, kind="ExternalInput").ap()

    def Internal(name, shape, dtype=F32):
        return nc.dram_tensor(name, list(shape), dtype, kind="Internal").ap()

    x_in = In("x_in", [NCH, 128, D])
    ident_h = In("ident", [128, 128])
    triu_h = In("triu", [128, 128])
    wfcT = In("wfcT", [32, 8, 128, 128])
    wprojT = In("wprojT", [32, 128, D])
    w1T = In("w1T", [32, 8, 128, 128])
    w2T = In("w2T", [32, 128, D])
    routerT = In("routerT", [8, 128, 16])
    rb_rep = In("rb_rep", [128, 16])
    b1cols = In("b1cols", [128, 32])
    b2rep = In("b2rep", [128, D])
    wqkvT = In("wqkvT", [3, 8, 8, 128, 128])
    wo_r = In("wo_r", [8, 128, D])
    awqkvT = In("awqkvT", [3, 8, 8, 128, 128])
    awoT = In("awoT", [8, 128, D])
    ttt_W0 = In("ttt_W0", [NPAIR, 128, 64])
    ttt_b0f = In("ttt_b0f", [64, NPAIR, 128])
    ttt_th = In("ttt_th", [128, NPAIR])
    ttt_tbf = In("ttt_tbf", [64, 2 * NPAIR])
    ttt_gf = In("ttt_gf", [64, NPAIR, 128])
    ttt_bef = In("ttt_bef", [64, NPAIR, 128])
    ttt_al = In("ttt_al", [64, 1])
    ttt_ik = In("ttt_ik", [64, 1])
    cvals_h = In("cvals", [128, 12])

    y_out = nc.dram_tensor("y", [NCH, 128, D], F32, kind="ExternalOutput").ap()

    x_spill = Internal("x_spill", [NCH, 128, D])
    wfcT_h = Internal("wfcT_h", [32, 8, 128, 128], BF16)
    wfcT_l = Internal("wfcT_l", [32, 8, 128, 128], BF16)
    wprojT_h = Internal("wprojT_h", [32, 128, D], BF16)
    wprojT_l = Internal("wprojT_l", [32, 128, D], BF16)
    w1T_h = Internal("w1T_h", [32, 8, 128, 128], BF16)
    w1T_l = Internal("w1T_l", [32, 8, 128, 128], BF16)
    w2T_h = Internal("w2T_h", [32, 128, D], BF16)
    w2T_l = Internal("w2T_l", [32, 128, D], BF16)
    routerT_h = Internal("routerT_h", [8, 128, 16], BF16)
    routerT_l = Internal("routerT_l", [8, 128, 16], BF16)
    wqkvT_h = Internal("wqkvT_h", [3, 8, 8, 128, 128], BF16)
    wqkvT_l = Internal("wqkvT_l", [3, 8, 8, 128, 128], BF16)
    awqkvT_h = Internal("awqkvT_h", [3, 8, 8, 128, 128], BF16)
    wo_r_h = Internal("wo_r_h", [8, 128, D], BF16)
    wo_r_l = Internal("wo_r_l", [8, 128, D], BF16)
    awoT_h = Internal("awoT_h", [8, 128, D], BF16)
    awoT_l = Internal("awoT_l", [8, 128, D], BF16)
    awqkvT_l = Internal("awqkvT_l", [3, 8, 8, 128, 128], BF16)
    cc_a_in = Internal("cc_a_in", [3, NPAIR, 128, TOK])
    cc_a_out = Internal("cc_a_out", [2, 3, NPAIR, 128, TOK])
    cc_b_in = Internal("cc_b_in", [NPAIR, 128, TOK])
    cc_b_out = Internal("cc_b_out", [2, NPAIR, 128, TOK])
    own_a = Internal("own_a", [3, NPAIR, 128, TOK])
    kv1 = Internal("kv1", [3, NPAIR, 128, 2 * TOK])
    own_b = Internal("own_b", [NPAIR, 128, 2 * TOK])
    ocanon = Internal("ocanon", [8, 128, TOK])

    sbuf = lambda name, shape, dtype=F32: nc.alloc_sbuf_tensor(name, list(shape), dtype).ap()
    X = sbuf("X", [128, NCH, D])
    BIG = sbuf("BIG", [128, 8, TOK])
    WBUF = sbuf("WBUF", [128, 8192])
    ident = sbuf("ident_s", [128, 128])
    triu = sbuf("triu_s", [128, 128])
    xn = sbuf("xn", [128, 1, D])
    stats = sbuf("stats", [128, 128])
    hq = sbuf("hq", [128, 1024])
    hr = sbuf("hr", [128, 512])
    stage = sbuf("stage", [128, 1024])
    maskb = sbuf("maskb", [128, NCH, 16])
    tcol = sbuf("tcol", [128, 32])
    b1s = sbuf("b1s", [128, 32])
    attnb = sbuf("attnb", [128, 512])
    b2s = sbuf("b2s", [128, D])
    sA = BIG[:, 0, 0:1024]
    sB = BIG[:, 1, 0:1024]
    sC = BIG[:, 2, 0:1024]
    sD = stage[:, 0:1024]
    tsc = BIG[:, 3, 0:1024].rearrange("p (c d) -> p c d", d=64)
    Wst = sbuf("Wst", [128, NPAIR, 64])
    bstf = sbuf("bstf", [64, NPAIR, 128])
    tgf = sbuf("tgf", [64, NPAIR, 128])
    tbef = sbuf("tbef", [64, NPAIR, 128])
    tth = sbuf("tth", [128, NPAIR])
    ttbf = sbuf("ttbf", [64, 2 * NPAIR])
    eta0 = sbuf("eta0", [64, 1])
    tmw = BIG[:, 3, :]
    odsb = BIG[:, 4, 0:64]
    WB16 = WBUF.bitcast(BF16)
    BIG16 = BIG.bitcast(BF16)
    XTh = BIG16[:, :, 0:2048]
    XTl = BIG16[:, :, 2048:4096]
    qkvb = WBUF[:, 0:6144].rearrange("p (q r t) -> p q r t", q=3, r=NPAIR)
    kpb = WBUF[:, 0:4096]
    vpb = WBUF[:, 4096:8192]
    qpb = BIG[:, 4:6, :].rearrange("p a t -> p (a t)")
    vtk = BIG[:, 0:3, :].rearrange("p a t -> p (a t)")[:, 0:4160] \
        .rearrange("p (c t) -> p c t", t=130)

    PS = [nc.alloc_psum_tensor(f"ps{i}", [128, 512], F32).ap() for i in range(8)]

    cvals = sbuf("cvals_s", [128, 12])
    _CONSTS = [EPS_LN, EPS_TTT, -1.0, 1.0, 0.0, 0.5, 0.1, 0.9, 0.7071067811865476]
    for _i, _v in enumerate(_CONSTS):
        nc.const_aps.register(F32, _v, cvals[:, _i:_i + 1])

    p = Plan()
    semnames = ["sIN", "sWT", "sT", "sV", "sS", "sG", "sSY", "sCC", "sOUT"]

    p.dma("sync", cvals[:], cvals_h, "sIN")
    p.dma("sync", b1s[:], b1cols, "sIN")
    p.dma("sync", b2s[:], b2rep, "sIN")
    p.dma("sync", X[:], x_in.rearrange("c p d -> p c d"), "sIN")
    p.dma("sync", ident[:], ident_h, "sIN")
    p.dma("sync", triu[:], triu_h, "sIN")
    p.dma("sync", Wst[:], ttt_W0.rearrange("q p d -> p q d"), "sIN")
    p.dma("sync", bstf[:], ttt_b0f.rearrange("t q d -> t q d"), "sIN")
    p.dma("sync", tgf[:], ttt_gf, "sIN")
    p.dma("sync", tbef[:], ttt_bef, "sIN")
    p.dma("sync", tth[:], ttt_th, "sIN")
    p.dma("sync", ttbf[:], ttt_tbf, "sIN")
    p.dma("sync", tcol[0:64, 0:1], ttt_al, "sIN")
    p.dma("sync", tcol[0:64, 1:2], ttt_ik, "sIN")
    p.wait("vector", "sIN", p.val("sIN"))
    p.raw("vector", "tensor_tensor", eta0[:], tcol[0:64, 0:1], tcol[0:64, 1:2], OP.add)
    p.drain("vector")
    p.raw("vector", "tensor_scalar_max", eta0[:], eta0[:], 0.0)
    p.bar("vector", "sV")

    def pregen():
        stg = WBUF[:, 0:4096]
        hi_t = WB16[:, 8192:12288]
        lo_t = WB16[:, 12288:16384]
        jobs = [(wfcT, wfcT_h, wfcT_l), (wprojT, wprojT_h, wprojT_l),
                (w1T, w1T_h, w1T_l), (w2T, w2T_h, w2T_l),
                (routerT, routerT_h, routerT_l),
                (wqkvT, wqkvT_h, wqkvT_l), (awqkvT, awqkvT_h, awqkvT_l),
                (wo_r, wo_r_h, wo_r_l), (awoT, awoT_h, awoT_l)]
        for srcw, dsth, dstl in jobs:
            n = 1
            for s in srcw.shape:
                n *= s
            cols = min(4096, n // 128)
            nt = n // (128 * cols)
            sf = srcw.flatten().rearrange("(t p c) -> t p c", p=128, c=cols)
            hf = dsth.flatten().rearrange("(t p c) -> t p c", p=128, c=cols)
            lf = dstl.flatten().rearrange("(t p c) -> t p c", p=128, c=cols)
            for t in range(nt):
                p.wait("sync", "sSY", p.val("sSY"))
                ld = p.dma("sync", stg[:, 0:cols], sf[t], "sSY")
                p.wait("vector", "sSY", ld)
                p.raw("vector", "tensor_copy", hi_t[:, 0:cols], stg[:, 0:cols])
                p.drain("vector")
                p.raw("vector", "scalar_tensor_tensor", lo_t[:, 0:cols],
                      stg[:, 0:cols], 1.0, hi_t[:, 0:cols], OP.mult, OP.subtract)
                vd = p.bar("vector", "sV")
                p.wait("sync", "sV", vd)
                p.dma("sync", hf[t], hi_t[:, 0:cols], "sSY")
                p.dma("sync", lf[t], lo_t[:, 0:cols], "sSY")

    pregen()

    def ln_to_xt():
        p.raw("vector", "reduce_sum", stats[:, 16:32], X[:], axis=AX)
        vsum = p.bar("vector", "sV")
        p.wait("scalar", "sV", vsum)
        for c in range(NCH):
            p.raw("scalar", "activation", sD, X[:, c, :], AF.Square,
                  bias=0.0, scale=1.0, accum_out=stats[:, 32 + c:33 + c])
            p.drain("scalar")
        ssq = p.bar("scalar", "sS")
        p.wait("vector", "sS", ssq)
        p.raw("vector", "tensor_scalar_mul", stats[:, 48:64], stats[:, 16:32], 1.0 / D)
        p.raw("vector", "tensor_scalar_mul", stats[:, 64:80], stats[:, 32:48], 1.0 / D)
        p.drain("vector")
        p.raw("vector", "tensor_tensor", stats[:, 80:96], stats[:, 48:64],
              stats[:, 48:64], OP.mult)
        p.drain("vector")
        p.raw("vector", "tensor_tensor", stats[:, 96:112], stats[:, 80:96],
              stats[:, 64:80], OP.subtract)
        vvar = p.bar("vector", "sV")
        p.wait("scalar", "sV", vvar)
        p.raw("scalar", "activation", stats[:, 112:128], stats[:, 96:112], AF.Sqrt,
              bias=EPS_LN, scale=-1.0)
        sstd = p.bar("scalar", "sS")
        p.wait("vector", "sS", sstd)
        p.raw("vector", "reciprocal", stats[:, 16:32], stats[:, 112:128])
        p.raw("vector", "tensor_scalar_mul", stats[:, 48:64], stats[:, 48:64], -1.0)
        p.bar("vector", "sV")
        tr_done = 0
        for c in range(NCH):
            if c >= 1:
                p.wait("vector", "sT", tr_done)
            p.raw("vector", "tensor_scalar", xn[:, 0, :], X[:, c, :],
                  stats[:, 48 + c:49 + c], stats[:, 16 + c:17 + c], OP.add, OP.mult)
            vapp = p.bar("vector", "sV")
            p.wait("tensor", "sV", vapp)
            for w in range(2):
                for d4 in range(4):
                    dk = w * 4 + d4
                    p.raw("tensor", "transpose", PS[d4][:, 0:128],
                          xn[:, 0, dk * 128:(dk + 1) * 128], ident[:])
                td = p.bar("tensor", "sT")
                tr_done = td
                p.wait("vector", "sT", td)
                for d4 in range(4):
                    dk = w * 4 + d4
                    p.raw("vector", "tensor_copy", XTh[:, dk, c * 128:(c + 1) * 128],
                          PS[d4][:, 0:128])
                p.drain("vector")
                for d4 in range(4):
                    dk = w * 4 + d4
                    p.raw("vector", "scalar_tensor_tensor",
                          XTl[:, dk, c * 128:(c + 1) * 128], PS[d4][:, 0:128], 1.0,
                          XTh[:, dk, c * 128:(c + 1) * 128], OP.mult, OP.subtract)
                ev = p.bar("vector", "sV")
                p.wait("tensor", "sV", ev)
        return p.bar("vector", "sV")

    def ffn(layer):
        is_l1 = layer == "L1"
        wfch, wfcl = (w1T_h, w1T_l) if is_l1 else (wfcT_h, wfcT_l)
        wpjh, wpjl = (w2T_h, w2T_l) if is_l1 else (wprojT_h, wprojT_l)
        hq16 = hq.bitcast(BF16)
        if is_l1:
            p.wait("sync", "sT", p.val("sT"))
            p.wait("sync", "sV", p.val("sV"))
            p.wait("sync", "sSY", p.val("sSY"))
            p.dma("sync", WB16[:, 0:128].rearrange("p (k n) -> p k n", n=16),
                  routerT_h.rearrange("k p n -> p k n"), "sWT")
            p.dma("sync", WB16[:, 128:256].rearrange("p (k n) -> p k n", n=16),
                  routerT_l.rearrange("k p n -> p k n"), "sWT")
            p.dma("sync", WBUF[:, 128:144], rb_rep, "sWT")
            wtv = p.val("sWT")
            p.wait("tensor", "sWT", wtv)
            rth = WB16[:, 0:128].rearrange("p (k n) -> p k n", n=16)
            rtl = WB16[:, 128:256].rearrange("p (k n) -> p k n", n=16)
            for c in range(NCH):
                psl = PS[4][:, 0:16]
                p.wait("tensor", "sV", p.val("sV"))
                nmm = 0
                for k in range(8):
                    for xa, wa in ((XTh, rth), (XTh, rtl), (XTl, rth)):
                        nmm += 1
                        p.raw("tensor", "matmul", psl,
                              xa[:, k, c * 128:(c + 1) * 128], wa[:, k, :],
                              start=(nmm == 1), stop=(nmm == 24))
                tl = p.bar("tensor", "sT")
                p.wait("vector", "sT", tl)
                p.raw("vector", "tensor_tensor", maskb[:, c, :], psl,
                      WBUF[:, 128:144], OP.add)
                p.drain("vector")
                l2 = stats[:, 32:48]
                m1 = stats[:, 48:49]
                p.raw("vector", "tensor_copy", l2, maskb[:, c, :])
                p.drain("vector")
                for it in range(3):
                    p.raw("vector", "reduce_max", m1, l2, axis=AX)
                    p.drain("vector")
                    p.raw("vector", "tensor_scalar", stats[:, 49:65], l2,
                          m1[:, 0:1], None, OP.is_ge)
                    p.drain("vector")
                    p.raw("vector", "scalar_tensor_tensor", l2, stats[:, 49:65],
                          -1e30, l2, OP.mult, OP.add)
                    p.drain("vector")
                p.raw("vector", "reduce_max", m1, l2, axis=AX)
                p.drain("vector")
                p.raw("vector", "tensor_scalar", maskb[:, c, :], maskb[:, c, :],
                      m1[:, 0:1], None, OP.is_ge)
                p.bar("vector", "sV")

        p.wait("sync", "sV", p.val("sV"))
        p.wait("sync", "sSY", p.val("sSY"))
        sp = p.dma("sync", x_spill.rearrange("c p d -> p c d"), X[:], "sSY")
        p.wait("vector", "sSY", sp)
        p.wait("tensor", "sSY", sp)

        fcslab = lambda lo: WB16[:, (4096 if lo else 0):(8192 if lo else 4096)]
        pjslab = lambda lo: WB16[:, (12288 if lo else 8192):(16384 if lo else 12288)]
        for fg in range(8):
            p.wait("sync", "sT", p.val("sT"))
            p.wait("sync", "sV", p.val("sV"))
            f0 = fg * 4
            p.dma("sync", fcslab(0).rearrange("p (j k n) -> p j k n", j=4, n=128),
                  wfch[f0:f0 + 4].rearrange("j k p n -> p j k n"), "sWT")
            p.dma("sync", fcslab(1).rearrange("p (j k n) -> p j k n", j=4, n=128),
                  wfcl[f0:f0 + 4].rearrange("j k p n -> p j k n"), "sWT")
            p.dma("sync", pjslab(0).rearrange("p (j n) -> p j n", j=4),
                  wpjh[f0:f0 + 4].rearrange("j p n -> p j n"), "sWT")
            p.dma("sync", pjslab(1).rearrange("p (j n) -> p j n", j=4),
                  wpjl[f0:f0 + 4].rearrange("j p n -> p j n"), "sWT")
            wdone = p.val("sWT")
            for tt in range(8):
                act_done = {}
                nblk = 2 if is_l1 else 1
                for blk in range(nblk):
                    jlist = list(range(blk * (4 // nblk), (blk + 1) * (4 // nblk)))

                    def emit_fc(j):
                        f = f0 + j
                        psh = PS[4 + (j % 2)][:, 0:256]
                        if j == 0:
                            p.wait("tensor", "sWT", wdone)
                        if j == jlist[0]:
                            p.wait("tensor", "sV", p.val("sV"))
                        elif j - 2 in act_done:
                            p.wait("tensor", "sV", act_done[j - 2])
                        nmm = 0
                        for k in range(8):
                            for xa, wsl in ((XTh, 0), (XTh, 1), (XTl, 0)):
                                nmm += 1
                                wt = fcslab(wsl)[:, j * 1024 + k * 128:
                                                 j * 1024 + (k + 1) * 128]
                                p.raw("tensor", "matmul", psh, wt,
                                      xa[:, k, tt * 256:(tt + 1) * 256],
                                      start=(nmm == 1), stop=(nmm == 24))
                        tfc = p.bar("tensor", "sT")
                        p.wait("scalar", "sT", tfc)
                        if is_l1:
                            p.raw("scalar", "activation",
                                  hr[:, (j % 2) * 256:(j % 2) * 256 + 256], psh,
                                  AF.Relu, bias=b1s[:, f:f + 1], scale=1.0)
                        else:
                            p.raw("scalar", "activation",
                                  hr[:, (j % 2) * 256:(j % 2) * 256 + 256], psh,
                                  AF.Relu)
                        sd = p.bar("scalar", "sS")
                        p.wait("vector", "sS", sd)
                        rsl = hr[:, (j % 2) * 256:(j % 2) * 256 + 256]
                        asl = attnb[:, (j % 2) * 256:(j % 2) * 256 + 256]
                        hhi = hq16[:, (j % 2) * 256:(j % 2) * 256 + 256]
                        hlo = hq16[:, 512 + (j % 2) * 256:512 + (j % 2) * 256 + 256]
                        if is_l1:
                            p.raw("vector", "tensor_copy", hhi, rsl)
                            p.drain("vector")
                            p.raw("vector", "scalar_tensor_tensor", hlo, rsl, 1.0,
                                  hhi, OP.mult, OP.subtract)
                        else:
                            p.raw("vector", "tensor_tensor", asl, rsl, rsl, OP.mult)
                            p.drain("vector")
                            p.raw("vector", "tensor_copy", hhi, asl)
                            p.drain("vector")
                            p.raw("vector", "scalar_tensor_tensor", hlo, asl, 1.0,
                                  hhi, OP.mult, OP.subtract)
                        act_done[j] = p.bar("vector", "sV")

                    def emit_proj(ji, j):
                        p.wait("tensor", "sV", act_done[j])
                        hhi = hq16[:, (j % 2) * 256:(j % 2) * 256 + 256]
                        hlo = hq16[:, 512 + (j % 2) * 256:512 + (j % 2) * 256 + 256]
                        first = ji == 0
                        last = ji == len(jlist) - 1
                        nps = 0
                        for ha, wsl in ((hhi, 0), (hhi, 1), (hlo, 0)):
                            nps += 1
                            for sub in range(2):
                                lhs = ha[:, sub * 128:(sub + 1) * 128]
                                for half in range(2):
                                    p.raw("tensor", "matmul", PS[sub * 2 + half],
                                          lhs,
                                          pjslab(wsl)[:, j * 1024 + half * 512:
                                                      j * 1024 + (half + 1) * 512],
                                          start=(first and nps == 1),
                                          stop=(last and nps == 3))

                    for ji, j in enumerate(jlist):
                        emit_fc(j)
                        if ji >= 1:
                            emit_proj(ji - 1, jlist[ji - 1])
                    emit_proj(len(jlist) - 1, jlist[-1])
                    tp = p.bar("tensor", "sT")
                    p.wait("vector", "sT", tp)
                    for sub in range(2):
                        c = tt * 2 + sub
                        for half in range(2):
                            dst = X[:, c, half * 512:(half + 1) * 512]
                            srcp = PS[sub * 2 + half]
                            if is_l1:
                                B = fg * 2 + blk
                                mcol = maskb[:, c, B:B + 1]
                                if fg == 0 and blk == 0:
                                    p.raw("vector", "tensor_scalar", dst, srcp,
                                          mcol, None, OP.mult)
                                else:
                                    p.raw("vector", "scalar_tensor_tensor", dst,
                                          srcp, mcol, dst, OP.mult, OP.add)
                            else:
                                if fg == 0:
                                    p.raw("vector", "tensor_copy", dst, srcp)
                                else:
                                    p.raw("vector", "tensor_tensor", dst, srcp,
                                          dst, OP.add)
                    p.bar("vector", "sV")
        for c in range(NCH):
            p.wait("sync", "sV", p.val("sV"))
            p.wait("sync", "sSY", p.val("sSY"))
            ld = p.dma("sync", sD, x_spill[c], "sSY")
            p.wait("vector", "sSY", ld)
            Yc = X[:, c, :]
            if is_l1:
                p.raw("vector", "tensor_tensor", sA, Yc, b2s[:], OP.add)
                p.drain("vector")
                p.raw("vector", "scalar_tensor_tensor", Yc, sA, 0.5, sD,
                      OP.mult, OP.add)
                p.bar("vector", "sV")
            else:
                p.raw("vector", "tensor_single_scalar", sB, Yc, 1.0, OP.is_gt)
                p.drain("vector")
                p.raw("vector", "tensor_copy", sC, sB)
                p.raw("vector", "tensor_copy", sA, Yc)
                p.drain("vector")
                for step in range(4):
                    p.raw("vector", "scalar_tensor_tensor", hr[:], sA[:, 0:512],
                          0.9, Yc[:, 0:512], OP.mult, OP.add)
                    p.raw("vector", "scalar_tensor_tensor", hq[:, 0:512],
                          sA[:, 512:1024], 0.9, Yc[:, 512:1024], OP.mult, OP.add)
                    p.drain("vector")
                    p.raw("vector", "tensor_tensor", sA[:, 0:512], hr[:],
                          sB[:, 0:512], OP.subtract)
                    p.raw("vector", "tensor_tensor", sA[:, 512:1024], hq[:, 0:512],
                          sB[:, 512:1024], OP.subtract)
                    p.drain("vector")
                    p.raw("vector", "tensor_single_scalar", sB, sA, 1.0, OP.is_gt)
                    p.drain("vector")
                    p.raw("vector", "tensor_tensor", sC, sC, sB, OP.add)
                    p.drain("vector")
                p.raw("vector", "scalar_tensor_tensor", Yc, sC, 0.1, sD,
                      OP.mult, OP.add)
                p.bar("vector", "sV")

    RG = [[0, 1], [2, 3], [4, 5], [6, 7]]

    def mixer_proj(wsrc, qscale=False):
        wh, wl = wsrc
        whv = WB16[:, 0:1024].rearrange("p (k n) -> p k n", n=128)
        wlv = WB16[:, 1024:2048].rearrange("p (k n) -> p k n", n=128)
        p.wait("sync", "sSY", p.val("sSY"))
        p.wait("sync", "sCC", p.val("sCC"))
        for t in range(3):
            for f in range(8):
                p.wait("sync", "sT", p.val("sT"))
                p.dma("sync", whv, wh[t, f].rearrange("k p n -> p k n"), "sWT")
                p.dma("sync", wlv, wl[t, f].rearrange("k p n -> p k n"), "sWT")
                p.wait("tensor", "sWT", p.val("sWT"))
                for tt in range(4):
                    ps = PS[tt % 2]
                    p.wait("tensor", "sV", p.val("sV"))
                    nmm = 0
                    for k in range(8):
                        for xa, wa in ((XTh, whv), (XTh, wlv), (XTl, whv)):
                            nmm += 1
                            p.raw("tensor", "matmul", ps, wa[:, k, :],
                                  xa[:, k, tt * 512:(tt + 1) * 512],
                                  start=(nmm == 1), stop=(nmm == 24))
                    tdd = p.bar("tensor", "sT")
                    p.wait("vector", "sT", tdd)
                    p.wait("vector", "sSY", p.val("sSY"))
                    slot = stage[:, (tt % 2) * 512:(tt % 2) * 512 + 512]
                    if qscale and t == 0:
                        p.raw("vector", "tensor_scalar_mul", slot, ps, 0.125)
                    else:
                        p.raw("vector", "tensor_copy", slot, ps)
                    vdd = p.bar("vector", "sV")
                    p.wait("sync", "sV", vdd)
                    dst = own_a[t, f] if f < 4 else cc_a_in[t, f - 4]
                    p.dma("sync", dst[:, tt * 512:(tt + 1) * 512], slot, "sSY")

    def exchange_a():
        p.wait("gpsimd", "sSY", p.val("sSY"))
        p.wait("gpsimd", "sCC", p.val("sCC"))

        def do_cc(e, sems):
            e.collective_compute("AllGather", OP.bypass, replica_groups=RG,
                                 ins=[cc_a_in], outs=[cc_a_out]) \
             .then_inc(sems["sCC"], 1)
        p.custom("gpsimd", do_cc, inc=(("sCC", 1),))
        p.wait("sync", "sCC", p.val("sCC"))
        p.wait("sync", "sSY", p.val("sSY"))

        def assemble(e, sems):
            for arm in nc.switch_core_id([nc.sync], n=128):
                half = int(arm.logical) % 2
                e.dma_start(out=kv1[:, :, :, half * TOK:(half + 1) * TOK],
                            in_=own_a).then_inc(sems["sSY"], 16)
                e.dma_start(out=kv1[:, :, :, (1 - half) * TOK:(2 - half) * TOK],
                            in_=cc_a_out[1 - half]).then_inc(sems["sSY"], 16)
        p.custom("sync", assemble, inc=(("sSY", 32),))
        return p.val("sSY")

    def exchange_b():
        p.wait("gpsimd", "sSY", p.val("sSY"))
        p.wait("gpsimd", "sCC", p.val("sCC"))
        p.wait("sync", "sSY", p.val("sSY"))
        p.wait("sync", "sCC", p.val("sCC"))

        def split(e, sems):
            for arm in nc.switch_core_id([nc.sync], n=128):
                half = int(arm.logical) % 2
                e.dma_start(out=cc_b_in,
                            in_=own_b[:, :, (1 - half) * TOK:(2 - half) * TOK]) \
                 .then_inc(sems["sSY"], 16)
                e.dma_start(out=ocanon[0:4].rearrange("q p t -> q p t"),
                            in_=own_b[:, :, half * TOK:(half + 1) * TOK]) \
                 .then_inc(sems["sSY"], 16)
        p.custom("sync", split, inc=(("sSY", 32),))
        p.wait("gpsimd", "sSY", p.val("sSY"))

        def do_cc(e, sems):
            e.collective_compute("AllGather", OP.bypass, replica_groups=RG,
                                 ins=[cc_b_in], outs=[cc_b_out]) \
             .then_inc(sems["sCC"], 1)
        p.custom("gpsimd", do_cc, inc=(("sCC", 1),))
        p.wait("sync", "sCC", p.val("sCC"))
        p.wait("sync", "sSY", p.val("sSY"))

        def gather(e, sems):
            for arm in nc.switch_core_id([nc.sync], n=128):
                half = int(arm.logical) % 2
                e.dma_start(out=ocanon[4:8], in_=cc_b_out[1 - half]) \
                 .then_inc(sems["sSY"], 16)
        p.custom("sync", gather, inc=(("sSY", 16),))
        return p.val("sSY")

    def wo_proj(wsrc, scale_half):
        wh, wl = wsrc
        p.wait("sync", "sT", p.val("sT"))
        p.wait("sync", "sV", p.val("sV"))
        p.wait("sync", "sSY", p.val("sSY"))
        whv = WB16[:, 0:8192].rearrange("p (j d) -> p j d", d=1024)
        wlv = WB16[:, 8192:16384].rearrange("p (j d) -> p j d", d=1024)
        ldw = p.dma("sync", whv, wh.rearrange("j p d -> p j d"), "sWT")
        ldw = p.dma("sync", wlv, wl.rearrange("j p d -> p j d"), "sWT")
        for j in range(8):
            lo = p.dma("sync", stage[:], ocanon[j, :, 0:1024], "sSY")
            lo2 = p.dma("sync", attnb[:], ocanon[j, :, 1024:1536], "sSY")
            lo3 = p.dma("sync", hr[:], ocanon[j, :, 1536:2048], "sSY")
            p.wait("vector", "sSY", lo3)
            p.raw("vector", "tensor_copy", XTh[:, j, 0:1024], stage[:])
            p.raw("vector", "tensor_copy", XTh[:, j, 1024:1536], attnb[:])
            p.raw("vector", "tensor_copy", XTh[:, j, 1536:2048], hr[:])
            p.drain("vector")
            p.raw("vector", "scalar_tensor_tensor", XTl[:, j, 0:1024], stage[:],
                  1.0, XTh[:, j, 0:1024], OP.mult, OP.subtract)
            p.raw("vector", "scalar_tensor_tensor", XTl[:, j, 1024:1536], attnb[:],
                  1.0, XTh[:, j, 1024:1536], OP.mult, OP.subtract)
            p.raw("vector", "scalar_tensor_tensor", XTl[:, j, 1536:2048], hr[:],
                  1.0, XTh[:, j, 1536:2048], OP.mult, OP.subtract)
            vs = p.bar("vector", "sV")
            p.wait("sync", "sV", vs)
        vsp = p.val("sV")
        p.wait("tensor", "sV", vsp)
        p.wait("tensor", "sWT", p.val("sWT"))
        for c in range(NCH):
            p.wait("tensor", "sV", p.val("sV"))
            for half in range(2):
                nmm = 0
                for j in range(8):
                    for oa, wa in ((XTh, whv), (XTh, wlv), (XTl, whv)):
                        nmm += 1
                        p.raw("tensor", "matmul", PS[half],
                              oa[:, j, c * 128:(c + 1) * 128],
                              wa[:, j, half * 512:(half + 1) * 512],
                              start=(nmm == 1), stop=(nmm == 24))
            tdd = p.bar("tensor", "sT")
            p.wait("vector", "sT", tdd)
            for half in range(2):
                dst = X[:, c, half * 512:(half + 1) * 512]
                p.raw("vector", "tensor_tensor", dst, PS[half], dst, OP.add)
            p.bar("vector", "sV")

    qkv_view = WBUF[:, 0:6144].rearrange("p (q r t) -> p q r t", q=3, r=NPAIR)
    SL = lambda i: BIG[0:64, 2 + i // 4, (i % 4) * 512:(i % 4) * 512 + 512]
    S8 = lambda a: stats[0:64, a:a + 8]
    V8 = lambda ap: ap.rearrange("t (s d) -> t s d", d=64)
    odsb_b = BIG[:, 6, 0:256]
    bstf_f = bstf.rearrange("t q d -> t (q d)")
    tgf_f = tgf.rearrange("t q d -> t (q d)")
    tbef_f = tbef.rearrange("t q d -> t (q d)")
    Wst_f = Wst.rearrange("p q d -> p (q d)")

    def ttt_chunk(c8):
        c64 = slice(c8 * 64, (c8 + 1) * 64)
        xkt, xvt, z1, sq, xhat, B2, go, gxh, t4, gx, REP1, REP2 = \
            [SL(i) for i in range(12)]
        pr = sq

        p.wait("tensor", "sV", p.val("sV"))
        p.wait("tensor", "sS", p.val("sS"))
        for pair in range(4):
            p.raw("tensor", "transpose", PS[1][0:64, pair * 128:(pair + 1) * 128],
                  qkv_view[:, 1, pair, c64], ident[:])
        for pair in range(4):
            p.raw("tensor", "transpose", PS[2][0:64, pair * 128:(pair + 1) * 128],
                  qkv_view[:, 2, pair, c64], ident[:])
        for hh in range(2):
            p.drain("tensor")
            hs = slice(64 * hh, 64 * hh + 64)
            for pair in range(4):
                p.raw("tensor", "matmul", PS[6][0:64, 2 * pair + hh:2 * pair + hh + 1],
                      qkv_view[hs, 1, pair, c64], tth[hs, pair:pair + 1],
                      start=True, stop=True)
                p.raw("tensor", "matmul",
                      PS[0][0:64, pair * 128 + hh * 64:pair * 128 + hh * 64 + 64],
                      qkv_view[hs, 1, pair, c64], Wst[hs, pair, :],
                      start=True, stop=True)
        tA = p.bar("tensor", "sT")

        p.wait("scalar", "sT", tA)
        p.raw("scalar", "copy", xkt, PS[1][0:64, 0:512])
        p.drain("scalar")
        p.raw("scalar", "copy", xvt, PS[2][0:64, 0:512])
        sE = p.bar("scalar", "sS")
        p.wait("vector", "sT", tA)
        p.raw("vector", "tensor_tensor", tcol[0:64, 16:24], PS[6][0:64, 0:8],
              ttbf[:], OP.add)
        p.drain("vector")
        p.raw("vector", "tensor_tensor", z1, PS[0][0:64, 0:512], bstf_f, OP.add)
        v1 = p.bar("vector", "sV")
        p.wait("scalar", "sV", v1)
        p.raw("scalar", "activation", tcol[0:64, 24:32], tcol[0:64, 16:24], AF.Sigmoid)
        p.drain("scalar")
        p.raw("scalar", "activation", sq, z1, AF.Square)
        s1_ = p.bar("scalar", "sS")
        p.wait("vector", "sS", sE)
        p.raw("vector", "tensor_tensor", B2, xkt, xvt, OP.subtract)
        p.drain("vector")
        p.raw("vector", "reduce_sum", S8(0), V8(z1), axis=AX)
        p.raw("vector", "tensor_tensor", B2, B2, tbef_f, OP.add)
        p.wait("vector", "sS", s1_)
        p.drain("vector")
        p.raw("vector", "reduce_sum", S8(8), V8(sq), axis=AX)
        p.raw("vector", "tensor_scalar_mul", S8(16), S8(0), 1.0 / 64)
        p.drain("vector")
        p.raw("vector", "tensor_scalar_mul", S8(24), S8(8), 1.0 / 64)
        p.raw("vector", "tensor_tensor", S8(32), S8(16), S8(16), OP.mult)
        p.drain("vector")
        p.raw("vector", "tensor_tensor", S8(40), S8(32), S8(24), OP.subtract)
        v2 = p.bar("vector", "sV")
        p.wait("scalar", "sV", v2)
        p.raw("scalar", "activation", S8(48), S8(40), AF.Sqrt, bias=EPS_TTT,
              scale=-1.0)
        s2_ = p.bar("scalar", "sS")
        p.wait("vector", "sS", s2_)
        p.raw("vector", "reciprocal", S8(56), S8(48))
        p.raw("vector", "scalar_tensor_tensor", tcol[0:64, 16:24],
              tcol[0:64, 24:32], 1.0 / 64, eta0.broadcast_to((64, 8)),
              OP.mult, OP.mult)
        p.drain("vector")
        p.raw("vector", "tensor_copy", REP1,
              S8(16).unsqueeze(2).broadcast_to((64, 8, 64)))
        p.raw("vector", "tensor_copy", REP2,
              S8(56).unsqueeze(2).broadcast_to((64, 8, 64)))
        p.drain("vector")
        p.raw("vector", "tensor_tensor", xhat, z1, REP1, OP.subtract)
        p.drain("vector")
        p.raw("vector", "tensor_tensor", xhat, xhat, REP2, OP.mult)
        p.drain("vector")
        p.raw("vector", "tensor_tensor", go, xhat, tgf_f, OP.mult)
        p.drain("vector")
        p.raw("vector", "tensor_tensor", go, go, B2, OP.add)
        p.drain("vector")
        p.raw("vector", "tensor_tensor", gxh, go, tgf_f, OP.mult)
        p.drain("vector")
        p.raw("vector", "reduce_sum", S8(0), V8(gxh), axis=AX, negate=True)
        p.raw("vector", "tensor_tensor", pr, gxh, xhat, OP.mult)
        p.drain("vector")
        p.raw("vector", "reduce_sum", S8(8), V8(pr), axis=AX, negate=True)
        p.raw("vector", "tensor_scalar_mul", t4, gxh, 64.0)
        p.drain("vector")
        p.raw("vector", "tensor_copy", REP1,
              S8(0).unsqueeze(2).broadcast_to((64, 8, 64)))
        p.raw("vector", "tensor_copy", REP2,
              S8(8).unsqueeze(2).broadcast_to((64, 8, 64)))
        p.drain("vector")
        p.raw("vector", "tensor_tensor", t4, t4, REP1, OP.add)
        p.raw("vector", "tensor_tensor", gx, xhat, REP2, OP.mult)
        p.drain("vector")
        p.raw("vector", "tensor_tensor", gx, gx, t4, OP.add)
        p.raw("vector", "tensor_scalar_mul", S8(48), S8(56), 1.0 / 64)
        p.drain("vector")
        p.raw("vector", "tensor_copy", REP1,
              S8(48).unsqueeze(2).broadcast_to((64, 8, 64)))
        p.raw("vector", "tensor_copy", REP2,
              tcol[0:64, 16:24].unsqueeze(2).broadcast_to((64, 8, 64)))
        p.drain("vector")
        p.raw("vector", "tensor_tensor", gx, gx, REP1, OP.mult)
        p.raw("vector", "tensor_tensor", xkt, xkt, REP2, OP.mult)
        vA = p.bar("vector", "sV")

        p.wait("tensor", "sV", vA)
        for pair in range(4):
            for hh in range(2):
                seg = slice((2 * pair + hh) * 64, (2 * pair + hh) * 64 + 64)
                p.raw("tensor", "matmul",
                      PS[4][64 * hh:64 * hh + 64, pair * 64:(pair + 1) * 64],
                      xkt[:, seg], gx[:, seg], start=True, stop=True)
                p.raw("tensor", "matmul", PS[5][0:64, seg],
                      REP2[:, seg], gx[:, seg], start=True, stop=True)
        tB = p.bar("tensor", "sT")
        p.wait("vector", "sT", tB)
        p.raw("vector", "tensor_tensor", Wst_f, Wst_f, PS[4][:, 0:256], OP.subtract)
        p.raw("vector", "tensor_tensor", bstf_f, bstf_f, PS[5][0:64, 0:512],
              OP.subtract)
        vB = p.bar("vector", "sV")

        p.wait("tensor", "sV", vB)
        for hh in range(2):
            if hh:
                p.drain("tensor")
            hs = slice(64 * hh, 64 * hh + 64)
            for pair in range(4):
                p.raw("tensor", "matmul",
                      PS[3][0:64, pair * 128 + hh * 64:pair * 128 + hh * 64 + 64],
                      qkv_view[hs, 0, pair, c64], Wst[hs, pair, :],
                      start=True, stop=True)
        tC = p.bar("tensor", "sT")

        p.wait("vector", "sT", tC)
        p.raw("vector", "tensor_tensor", z1, PS[3][0:64, 0:512], bstf_f, OP.add)
        v3 = p.bar("vector", "sV")
        p.wait("scalar", "sV", v3)
        p.raw("scalar", "activation", sq, z1, AF.Square)
        s3_ = p.bar("scalar", "sS")
        p.wait("vector", "sS", s3_)
        p.raw("vector", "reduce_sum", S8(0), V8(z1), axis=AX)
        p.raw("vector", "reduce_sum", S8(8), V8(sq), axis=AX)
        p.drain("vector")
        p.raw("vector", "tensor_scalar_mul", S8(16), S8(0), 1.0 / 64)
        p.raw("vector", "tensor_scalar_mul", S8(24), S8(8), 1.0 / 64)
        p.drain("vector")
        p.raw("vector", "tensor_tensor", S8(32), S8(16), S8(16), OP.mult)
        p.drain("vector")
        p.raw("vector", "tensor_tensor", S8(40), S8(32), S8(24), OP.subtract)
        v4 = p.bar("vector", "sV")
        p.wait("scalar", "sV", v4)
        p.raw("scalar", "activation", S8(48), S8(40), AF.Sqrt, bias=EPS_TTT,
              scale=-1.0)
        s4_ = p.bar("scalar", "sS")
        p.wait("vector", "sS", s4_)
        p.raw("vector", "reciprocal", S8(56), S8(48))
        p.drain("vector")
        p.raw("vector", "tensor_copy", REP1,
              S8(16).unsqueeze(2).broadcast_to((64, 8, 64)))
        p.raw("vector", "tensor_copy", REP2,
              S8(56).unsqueeze(2).broadcast_to((64, 8, 64)))
        p.drain("vector")
        p.raw("vector", "tensor_tensor", xhat, z1, REP1, OP.subtract)
        p.drain("vector")
        p.raw("vector", "tensor_tensor", xhat, xhat, REP2, OP.mult)
        p.drain("vector")
        p.raw("vector", "tensor_tensor", go, xhat, tgf_f, OP.mult)
        p.drain("vector")
        p.raw("vector", "tensor_tensor", go, go, tbef_f, OP.add)
        vC = p.bar("vector", "sV")

        p.wait("tensor", "sV", vC)
        for pair in range(4):
            p.raw("tensor", "transpose", PS[7][:, pair * 64:(pair + 1) * 64],
                  go[:, pair * 128:(pair + 1) * 128], ident[0:64, 0:64])
        tD = p.bar("tensor", "sT")
        p.wait("vector", "sT", tD)
        p.wait("vector", "sSY", p.val("sSY"))
        p.raw("vector", "tensor_tensor", odsb_b, PS[7][:, 0:256],
              qkv_view[:, 0, :, c64], OP.add)
        return p.bar("vector", "sV")

    def ttt_sublayer():
        ln_to_xt()
        mixer_proj((wqkvT_h, wqkvT_l))
        if tstage < 2:
            return
        exchange_a()
        if tstage < 3:
            return
        for g in range(8):
            p.wait("sync", "sT", p.val("sT"))
            p.wait("sync", "sV", p.val("sV"))
            p.wait("sync", "sSY", p.val("sSY"))
            ld = p.dma("sync", qkv_view,
                       kv1[:, :, :, g * 512:(g + 1) * 512]
                       .rearrange("q r p t -> p q r t"), "sSY")
            p.wait("tensor", "sSY", ld)
            p.wait("vector", "sSY", ld)
            for c8 in range(8):
                if titer is not None and g * 8 + c8 >= titer:
                    continue
                vD = ttt_chunk(c8)
                p.wait("sync", "sV", vD)
                c = g * 8 + c8
                p.dma("sync", own_b[:, :, c * 64:(c + 1) * 64]
                      .rearrange("q p t -> p q t"),
                      odsb_b.rearrange("p (r t) -> p r t", t=64), "sSY")
        if tstage < 4:
            return
        exchange_b()
        if tstage < 5:
            return
        wo_proj((wo_r_h, wo_r_l), False)

    RSQRT2 = 0.7071067811865476

    def attn_sublayer():
        ln_to_xt()
        mixer_proj((awqkvT_h, awqkvT_l), qscale=True)
        exchange_a()
        khi = WB16[:, 0:4096]
        klo = WB16[:, 4096:8192]
        qhi = WB16[:, 8192:12288]
        qlo = WB16[:, 12288:16384]
        vtk_h = BIG16[:, 0:2, :].rearrange("p a t -> p (a t)")[:, 0:4160] \
            .rearrange("p (c t) -> p c t", t=130)
        vtk_l = BIG16[:, 2:4, :].rearrange("p a t -> p (a t)")[:, 0:4160] \
            .rearrange("p (c t) -> p c t", t=130)
        fstg = BIG[:, 4:6, :].rearrange("p a t -> p (a t)")
        hq16 = hq.bitcast(BF16)
        p.wait("vector", "sT", p.val("sT"))
        p.wait("vector", "sV", p.val("sV"))
        triu16 = maskb.bitcast(BF16).rearrange("p a b -> p (a b)")[:, 0:128]
        p.raw("vector", "tensor_copy", triu16, triu[:])
        p.raw("vector", "memset", vtk_h[:, :, 64:65], 1.0)
        p.raw("vector", "memset", vtk_h[:, :, 129:130], 1.0)
        p.raw("vector", "memset", vtk_l[:, :, 64:65], 0.0)
        p.raw("vector", "memset", vtk_l[:, :, 129:130], 0.0)
        p.bar("vector", "sV")
        for pair in range(NPAIR):
            p.wait("sync", "sT", p.val("sT"))
            p.wait("sync", "sV", p.val("sV"))
            p.wait("sync", "sSY", p.val("sSY"))
            ldk = p.dma("sync", fstg, kv1[1, pair], "sSY")
            p.wait("vector", "sSY", ldk)
            p.raw("vector", "tensor_copy", khi, fstg)
            p.drain("vector")
            p.raw("vector", "scalar_tensor_tensor", klo, fstg, 1.0, khi,
                  OP.mult, OP.subtract)
            vk = p.bar("vector", "sV")
            p.wait("sync", "sV", vk)
            ldq = p.dma("sync", fstg, kv1[0, pair], "sSY")
            p.wait("vector", "sSY", ldq)
            p.raw("vector", "tensor_copy", qhi, fstg)
            p.drain("vector")
            p.raw("vector", "scalar_tensor_tensor", qlo, fstg, 1.0, qhi,
                  OP.mult, OP.subtract)
            vq = p.bar("vector", "sV")
            p.wait("sync", "sV", vq)
            ldv = p.dma("sync", fstg, kv1[2, pair], "sSY")
            p.wait("tensor", "sSY", ldv)
            for kb in range(32):
                ps = PS[7][:, (kb % 2) * 256:(kb % 2) * 256 + 128]
                p.wait("tensor", "sV", p.val("sV"))
                p.raw("tensor", "transpose", ps, fstg[:, kb * 128:(kb + 1) * 128],
                      ident[:])
                td = p.bar("tensor", "sT")
                p.wait("vector", "sT", td)
                p.raw("vector", "tensor_copy", vtk_h[:, kb, 0:64], ps[:, 0:64])
                p.raw("vector", "tensor_copy", vtk_h[:, kb, 65:129], ps[:, 64:128])
                p.drain("vector")
                p.raw("vector", "scalar_tensor_tensor", vtk_l[:, kb, 0:64],
                      ps[:, 0:64], 1.0, vtk_h[:, kb, 0:64], OP.mult, OP.subtract)
                p.raw("vector", "scalar_tensor_tensor", vtk_l[:, kb, 65:129],
                      ps[:, 64:128], 1.0, vtk_h[:, kb, 65:129], OP.mult, OP.subtract)
                p.bar("vector", "sV")
            for qt in range(8):
                nkb = (qt + 1) * 4
                for hh in range(2):
                    hb = 64 * hh
                    split_done = {}
                    sq_done = {}

                    def emit_scores(kb):
                        ps_s = PS[6 if kb % 2 == 0 else 5]
                        if kb >= 2:
                            p.wait("tensor", "sV", split_done[kb - 2])
                            p.wait("tensor", "sS", sq_done[kb - 2])
                        nmm = 0
                        for ka, qa in ((khi, qhi), (khi, qlo), (klo, qhi)):
                            nmm += 1
                            p.raw("tensor", "matmul", ps_s,
                                  ka[hb:hb + 64, kb * 128:(kb + 1) * 128],
                                  qa[hb:hb + 64, qt * 512:(qt + 1) * 512],
                                  start=(nmm == 1), stop=(nmm == 3))
                        ts = p.bar("tensor", "sT")
                        p.wait("scalar", "sT", ts)
                        fsl = attnb[:] if kb % 2 == 0 else hr[:]
                        p.raw("scalar", "activation", fsl, ps_s, AF.Square,
                              bias=RSQRT2, scale=RSQRT2)
                        sq_done[kb] = p.bar("scalar", "sS")
                        p.wait("vector", "sS", sq_done[kb])
                        ahi = hq16[:, (kb % 2) * 1024:(kb % 2) * 1024 + 512]
                        alo = hq16[:, (kb % 2) * 1024 + 512:(kb % 2) * 1024 + 1024]
                        p.raw("vector", "tensor_scalar_add", ahi, fsl, 0.5)
                        p.drain("vector")
                        p.raw("vector", "scalar_tensor_tensor", alo, fsl, 0.5,
                              ahi, OP.add, OP.subtract)
                        if kb >= qt * 4:
                            sub = kb - qt * 4
                            p.drain("vector")
                            p.raw("vector", "tensor_tensor",
                                  ahi[:, sub * 128:(sub + 1) * 128],
                                  ahi[:, sub * 128:(sub + 1) * 128], triu16, OP.mult)
                            p.raw("vector", "tensor_tensor",
                                  alo[:, sub * 128:(sub + 1) * 128],
                                  alo[:, sub * 128:(sub + 1) * 128], triu16, OP.mult)
                        split_done[kb] = p.bar("vector", "sV")

                    def emit_av(kb):
                        p.wait("tensor", "sV", split_done[kb])
                        ahi = hq16[:, (kb % 2) * 1024:(kb % 2) * 1024 + 512]
                        alo = hq16[:, (kb % 2) * 1024 + 512:(kb % 2) * 1024 + 1024]
                        sub0 = max(0, kb - qt * 4)
                        for aa, vv in ((ahi, vtk_h), (ahi, vtk_l), (alo, vtk_h)):
                            ip = 0 if aa is ahi and vv is vtk_h else (1 if vv is vtk_l else 2)
                            for sub in range(sub0, 4):
                                p.raw("tensor", "matmul", PS[sub][:, 0:65],
                                      aa[:, sub * 128:(sub + 1) * 128],
                                      vv[:, kb, 65 * hh:65 * hh + 65],
                                      start=(kb == 0 and ip == 0),
                                      stop=(kb == qt * 4 + sub and ip == 2))

                    for kb in range(nkb):
                        emit_scores(kb)
                        if kb >= 1:
                            emit_av(kb - 1)
                    emit_av(nkb - 1)
                    tav = p.bar("tensor", "sT")
                    p.wait("vector", "sT", tav)
                    for sub in range(4):
                        p.raw("vector", "tensor_scalar_add", tcol[:, 8 + sub:9 + sub],
                              PS[sub][:, 64:65], EPS_ATT)
                    p.drain("vector")
                    for sub in range(4):
                        p.raw("vector", "reciprocal", tcol[:, 12 + sub:13 + sub],
                              tcol[:, 8 + sub:9 + sub])
                    p.drain("vector")
                    for sub in range(4):
                        p.raw("vector", "tensor_scalar",
                              stage[:, sub * 128 + hh * 64:sub * 128 + hh * 64 + 64],
                              PS[sub][:, 0:64], tcol[:, 12 + sub:13 + sub], None,
                              OP.mult)
                    p.bar("vector", "sV")
                for sub in range(4):
                    p.wait("tensor", "sV", p.val("sV"))
                    p.raw("tensor", "transpose", PS[7][:, 256:384],
                          stage[:, sub * 128:(sub + 1) * 128], ident[:])
                    td = p.bar("tensor", "sT")
                    p.wait("vector", "sT", td)
                    p.wait("vector", "sSY", p.val("sSY"))
                    p.raw("vector", "tensor_copy", stage[:, 512:640], PS[7][:, 256:384])
                    vs = p.bar("vector", "sV")
                    p.wait("sync", "sV", vs)
                    p.dma("sync",
                          own_b[pair][:, qt * 512 + sub * 128:qt * 512 + (sub + 1) * 128],
                          stage[:, 512:640], "sSY")
        exchange_b()
        wo_proj((awoT_h, awoT_l), False)

    ln_to_xt()
    ffn("L0")
    if upto >= 2:
        ttt_sublayer()
    if upto >= 3:
        ln_to_xt()
        ffn("L0")
    if upto >= 4:
        ln_to_xt()
        ffn("L1")
    if upto >= 5:
        attn_sublayer()
    if upto >= 6:
        ln_to_xt()
        ffn("L1")

    vfin = p.bar("vector", "sV")
    p.wait("sync", "sV", vfin)
    p.dma("sync", y_out.rearrange("c p d -> p c d"), X[:], "sOUT")
    p.wait("sync", "sOUT", p.val("sOUT"))

    import contextlib
    S = {}
    with contextlib.ExitStack() as st:
        for sn in semnames:
            S[sn] = st.enter_context(nc.semaphore(sn))
        block = st.enter_context(nc.Block())

        @block.sync
        def _(e):
            _exec_ops(e, p.ops["sync"], S)

        @block.tensor
        def _(e):
            _exec_ops(e, p.ops["tensor"], S)

        @block.vector
        def _(e):
            _exec_ops(e, p.ops["vector"], S)

        @block.scalar
        def _(e):
            _exec_ops(e, p.ops["scalar"], S)

        @block.gpsimd
        def _(e):
            _exec_ops(e, p.ops["gpsimd"], S)

    return nc


def prep_inputs(inp):
    x = np.asarray(inp["L0_attn_hidden_states"], np.float32)

    def t32(a):
        return np.ascontiguousarray(np.asarray(a, np.float32))

    def fc_tiles(wT):
        Dd, Ff = wT.shape
        return np.ascontiguousarray(
            wT.reshape(Dd // 128, 128, Ff // 128, 128).transpose(2, 0, 1, 3))

    wfcT = fc_tiles(t32(inp["L0_ffn_W_fc"]).T)
    wprojT = np.ascontiguousarray(t32(inp["L0_ffn_W_proj"]).T.reshape(32, 128, D))
    w1T = fc_tiles(t32(inp["L1_ffn_w1"]).T)
    w2T = np.ascontiguousarray(t32(inp["L1_ffn_w2"]).T.reshape(32, 128, D))
    routerT = np.ascontiguousarray(t32(inp["L1_ffn_router_w"]).T.reshape(8, 128, 16))
    rb_rep = np.ascontiguousarray(
        np.broadcast_to(t32(inp["L1_ffn_router_b"])[None, :], (128, 16)))
    b1cols = np.ascontiguousarray(t32(inp["L1_ffn_b1"]).reshape(32, 128).T)
    b2rep = np.ascontiguousarray(np.broadcast_to(t32(inp["L1_ffn_b2"])[None, :], (128, D)))

    wq, wk, wv = t32(inp["L0_attn_wq"]), t32(inp["L0_attn_wk"]), t32(inp["L0_attn_wv"])
    wo = t32(inp["L0_attn_wo"])
    aWq, aWk, aWv = t32(inp["L1_attn_Wq"]), t32(inp["L1_attn_Wk"]), t32(inp["L1_attn_Wv"])
    aWo = t32(inp["L1_attn_Wo"])

    ident = np.eye(128, dtype=np.float32)
    triu = np.triu(np.ones((128, 128), np.float32))

    common = dict(wfcT=wfcT, wprojT=wprojT, w1T=w1T, w2T=w2T, routerT=routerT,
                  rb_rep=rb_rep, b1cols=b1cols, b2rep=b2rep, ident=ident, triu=triu)

    W1 = t32(inp["L0_attn_W1"])
    b1t = t32(inp["L0_attn_b1"]).reshape(16, 64)
    theta = t32(inp["L0_attn_theta"]).reshape(16, 64, 1)
    tb = t32(inp["L0_attn_theta_bias"]).reshape(16)
    alpha = t32(inp["L0_attn_alpha"])
    gamma = t32(inp["L0_attn_gamma"]).reshape(16, 64)
    beta = t32(inp["L0_attn_beta"]).reshape(16, 64)
    invk = (1.0 / np.arange(1, KCH + 1)).astype(np.float32)

    in_maps = []
    for c in range(NCORE):
        b, half = divmod(c, 2)
        hsel = list(range(8)) if half == 0 else list(range(8, 16))
        osel = [h for h in range(16) if h not in hsel]
        order = hsel + osel

        def headperm_cols(w):
            wr = w.reshape(w.shape[0], 16, 64)[:, order, :].reshape(w.shape[0], 1024)
            return fc_tiles(np.ascontiguousarray(wr))

        wqkvT = np.stack([headperm_cols(wq), headperm_cols(wk), headperm_cols(wv)])
        awqkvT = np.stack([headperm_cols(aWq.T), headperm_cols(aWk.T),
                           headperm_cols(aWv.T)])
        wo_r = np.ascontiguousarray(wo.reshape(16, 64, D)[order].reshape(8, 128, D))
        awoT = np.ascontiguousarray(aWo.T.reshape(16, 64, D)[order].reshape(8, 128, D))

        def pairs(a, rep=False):
            outs = []
            for pp in range(4):
                h0, h1 = hsel[2 * pp], hsel[2 * pp + 1]
                x0 = np.asarray(a[h0]); x1 = np.asarray(a[h1])
                if rep:
                    x0 = np.broadcast_to(x0.reshape(1, -1), (64, x0.size))
                    x1 = np.broadcast_to(x1.reshape(1, -1), (64, x1.size))
                if x0.ndim == 1:
                    x0 = x0.reshape(64, 1); x1 = x1.reshape(64, 1)
                outs.append(np.concatenate([x0, x1], 0))
            return np.ascontiguousarray(np.stack(outs)).astype(np.float32)

        ttt_W0 = pairs(W1)
        ttt_th = np.ascontiguousarray(pairs(theta).reshape(4, 128).T)

        def tmrows(a):
            out = np.zeros((64, 4, 128), np.float32)
            for pp in range(4):
                h0, h1 = hsel[2 * pp], hsel[2 * pp + 1]
                out[:, pp, 0:64] = a[h0][None, :]
                out[:, pp, 64:128] = a[h1][None, :]
            return np.ascontiguousarray(out)

        ttt_b0f = tmrows(b1t)
        ttt_gf = tmrows(gamma)
        ttt_bef = tmrows(beta)
        ttt_tbf = np.zeros((64, 8), np.float32)
        for pp in range(4):
            ttt_tbf[:, 2 * pp] = tb[hsel[2 * pp]]
            ttt_tbf[:, 2 * pp + 1] = tb[hsel[2 * pp + 1]]
        ttt_al = np.ascontiguousarray(alpha.reshape(64, 1))
        ttt_ik = np.ascontiguousarray(invk.reshape(64, 1))

        x_core = np.ascontiguousarray(
            x[b, half * TOK:(half + 1) * TOK].reshape(NCH, 128, D))

        m = dict(common)
        m["cvals"] = np.ascontiguousarray(np.broadcast_to(
            np.array([1e-5, 1e-6, -1.0, 1.0, 0.0, 0.5, 0.1, 0.9,
                      0.7071067811865476, 0.0, 0.0, 0.0], np.float32)[None, :],
            (128, 12)))
        m.update(x_in=x_core, wqkvT=wqkvT, wo_r=wo_r, awqkvT=awqkvT, awoT=awoT,
                 ttt_W0=ttt_W0, ttt_b0f=ttt_b0f, ttt_th=ttt_th, ttt_tbf=ttt_tbf,
                 ttt_gf=ttt_gf, ttt_bef=ttt_bef, ttt_al=ttt_al, ttt_ik=ttt_ik)
        in_maps.append(m)
    return in_maps


_CACHE = {}


def kernel(**inputs):
    in_maps = prep_inputs(inputs)
    key = (_UPTO, _TSTAGE, _TITER, _TPHASE, _TSEL)
    if key not in _CACHE:
        _CACHE[key] = build_program(_UPTO, _TSTAGE, _TITER, _TPHASE, _TSEL)
    nc = _CACHE[key]
    res = run_bass_kernel_spmd(nc, in_maps, core_ids=list(range(NCORE)))
    out = np.empty((4, 4096, D), np.float32)
    for c in range(NCORE):
        b, half = divmod(c, 2)
        out[b, half * TOK:(half + 1) * TOK] = \
            np.asarray(res.results[c]["y"]).reshape(TOK, D)
    return out
